# version 12
# baseline (speedup 1.0000x reference)
import contextlib
import numpy as np
import concourse.bass as bass
import concourse.mybir as mybir
from concourse.bass_utils import run_bass_kernel_spmd
from concourse.bass_core_rust import IndirectOffsetOnAxis

F32 = mybir.dt.float32
BF16 = mybir.dt.bfloat16
AF = mybir.ActivationFunctionType
ALU = mybir.AluOpType
AX = mybir.AxisListType.X

D = 1024
H = 16
HD = 64
FD = 64
B = 4
S = 4096
TC = 2048
NSUB = 4
TS = 512
DC = 8
NE = 16
MIDR = 256
NEX = 20

ENGS = ("tensor", "vector", "scalar", "sync", "gpsimd")


class Prog:
    def __init__(self):
        self.ops = {e: [] for e in ENGS}
        self.n = {e: 0 for e in ENGS}
        self.ncoll = 0

    def emit(self, eng, fn, wait=(), inc=1):
        self.ops[eng].append((fn, tuple(wait), inc, None))
        self.n[eng] += inc
        return (eng, self.n[eng])

    def emit_coll(self, fn, wait=()):
        self.ops["gpsimd"].append((fn, tuple(wait), 0, "coll"))
        self.ncoll += 1
        return ("coll", self.ncoll)

    def tok(self, eng):
        return (eng, self.n[eng])


def wl(*toks):
    return [t for t in toks if t is not None]


def build_program(debug=False, upto="fin"):
    nc = bass.Bass("TRN2", target_bir_lowering=False)

    xd = nc.dram_tensor("x", [NSUB, D, TS], F32, kind="ExternalInput").ap()
    wq_d = nc.dram_tensor("wq", [D, D], F32, kind="ExternalInput").ap()
    wk_d = nc.dram_tensor("wk", [D, D], F32, kind="ExternalInput").ap()
    wv_d = nc.dram_tensor("wv", [D, D], F32, kind="ExternalInput").ap()
    wo_d = nc.dram_tensor("wo", [D, D], F32, kind="ExternalInput").ap()
    wf_d = nc.dram_tensor("wf", [HD, FD], F32, kind="ExternalInput").ap()
    bf_d = nc.dram_tensor("bfv", [FD], F32, kind="ExternalInput").ap()
    bo_d = nc.dram_tensor("bov", [D], F32, kind="ExternalInput").ap()
    rw_d = nc.dram_tensor("rw", [D, NE], F32, kind="ExternalInput").ap()
    wgr_d = nc.dram_tensor("wg_r", [NE, D, MIDR], F32, kind="ExternalInput").ap()
    wur_d = nc.dram_tensor("wu_r", [NE, D, MIDR], F32, kind="ExternalInput").ap()
    wdr_d = nc.dram_tensor("wd_r", [NE, MIDR, D], F32, kind="ExternalInput").ap()
    wgs_d = nc.dram_tensor("wg_s", [2, D, 2 * MIDR], F32, kind="ExternalInput").ap()
    wus_d = nc.dram_tensor("wu_s", [2, D, 2 * MIDR], F32, kind="ExternalInput").ap()
    wds_d = nc.dram_tensor("wd_s", [2, 2 * MIDR, D], F32, kind="ExternalInput").ap()
    yd = nc.dram_tensor("y", [NSUB, DC, 128, TS], F32, kind="ExternalOutput").ap()
    xt_d = nc.dram_tensor("xt", [TC, D], F32, kind="ExternalInput").ap()
    CAP = 512
    xg_hbm = nc.dram_tensor("xg", [16 * CAP + 64, D], BF16)
    tok_hbm = nc.dram_tensor("tokh", [128 * 65], mybir.dt.int32)
    cg_hbm = nc.dram_tensor("cgh", [128 * 65], F32)
    rt_hbm = nc.dram_tensor("rt", [2 * (TC + 128), D], BF16)
    xnh = nc.dram_tensor("xnh", [D, TC], BF16)
    cc_in = nc.dram_tensor("cc_in", [128, 520], F32)
    cc_out = nc.dram_tensor("cc_out", [128, 520], F32)
    dbg = {}
    if debug:
        dbg["xn"] = nc.dram_tensor("dbg_xn", [DC, 128, TC], F32, kind="ExternalOutput").ap()
        dbg["qf"] = nc.dram_tensor("dbg_qf", [DC, 128, TC], F32, kind="ExternalOutput").ap()
        dbg["kvp"] = nc.dram_tensor("dbg_kvp", [128, 520], F32, kind="ExternalOutput").ap()
        dbg["kvs"] = nc.dram_tensor("dbg_kvs", [128, 520], F32, kind="ExternalOutput").ap()
        dbg["acc"] = nc.dram_tensor("dbg_acc", [DC, 128, TC], F32, kind="ExternalOutput").ap()
        dbg["comb"] = nc.dram_tensor("dbg_comb", [NE, TS], F32, kind="ExternalOutput").ap()
        dbg["lt"] = nc.dram_tensor("dbg_lt", [NE, TS], F32, kind="ExternalOutput").ap()
        dbg["c32"] = nc.dram_tensor("dbg_c32", [128, 64], F32, kind="ExternalOutput").ap()
        dbg["cst"] = nc.dram_tensor("dbg_cst", [128, 112], F32, kind="ExternalOutput").ap()
        dbg["ffn"] = nc.dram_tensor("dbg_ffn", [DC, 128, TC], F32, kind="ExternalOutput").ap()
        dbg["offb"] = nc.dram_tensor("dbg_offb", [16, 128, 8], F32, kind="ExternalOutput").ap()
        dbg["slots"] = nc.dram_tensor("dbg_slots", [16, 512], F32, kind="ExternalOutput").ap()
        dbg["stash"] = nc.dram_tensor("dbg_stash", [128, 256], F32, kind="ExternalOutput").ap()
        dbg["cstF"] = nc.dram_tensor("dbg_cstF", [128, 112], F32, kind="ExternalOutput").ap()
        dbg["rt"] = nc.dram_tensor("dbg_rt", [256, D], F32, kind="ExternalOutput").ap()
        dbg["xg"] = nc.dram_tensor("dbg_xg", [128, D], F32, kind="ExternalOutput").ap()
        dbg["cgh"] = nc.dram_tensor("dbg_cgh", [512], F32, kind="ExternalOutput").ap()
        dbg["tokh"] = nc.dram_tensor("dbg_tokh", [512], mybir.dt.int32, kind="ExternalOutput").ap()
        dbg["xntm"] = nc.dram_tensor("dbg_xntm", [128, D], F32, kind="ExternalOutput").ap()
        dbg["xtm"] = nc.dram_tensor("dbg_xtm", [128, D], F32, kind="ExternalOutput").ap()

    prog = Prog()

    with contextlib.ExitStack() as stack:
        E = stack.enter_context

        xn = E(nc.sbuf_tensor("xn", [128, DC * TC], BF16))
        qfa = E(nc.sbuf_tensor("qfa", [128, DC * TC], BF16))
        big = E(nc.sbuf_tensor("big", [128, 32768], BF16))
        scr = E(nc.sbuf_tensor("scr", [128, 20544], BF16))
        wf2 = E(nc.sbuf_tensor("wf2", [128, FD], BF16))
        wfblk = E(nc.sbuf_tensor("wfblk", [128, 128], BF16))
        rw_s = E(nc.sbuf_tensor("rw_s", [128, DC * NE], BF16))
        bf8 = E(nc.sbuf_tensor("bf8", [1, 512], BF16))
        kvbf = E(nc.sbuf_tensor("kvbf", [128, 520], BF16))
        ks2 = E(nc.sbuf_tensor("ks2", [128, 2 * DC], BF16))
        nrbf = E(nc.sbuf_tensor("nrbf", [2, 2 * 512], BF16))
        rowbf = nrbf
        nbsb = E(nc.sbuf_tensor("nbsb", [128, 2 * 512], BF16))
        hmask = E(nc.sbuf_tensor("hmask", [2, 128], BF16))
        ones1 = E(nc.sbuf_tensor("ones1", [1, 128], BF16))
        ones128 = E(nc.sbuf_tensor("ones128", [128, 1], BF16))
        cfm = E(nc.sbuf_tensor("cfm", [16, NSUB * TS], BF16))
        rb_sb = E(nc.sbuf_tensor("rb_sb", [128, 512], BF16))
        murb_sb = E(nc.sbuf_tensor("murb_sb", [128, 512], BF16))
        bf128 = E(nc.sbuf_tensor("bf128", [128, 1], F32))
        bo_sb = E(nc.sbuf_tensor("bo_sb", [128, DC], F32))
        kvsb = E(nc.sbuf_tensor("kvsb", [128, 520], F32))
        lt32 = E(nc.sbuf_tensor("lt32", [16, 512], F32))
        comb32 = E(nc.sbuf_tensor("comb32", [128, 64], F32))
        cst = E(nc.sbuf_tensor("cst", [128, 112], F32))
        ident32 = E(nc.sbuf_tensor("ident32", [128, 128], F32))
        pid32 = E(nc.sbuf_tensor("pid32", [128, 1], F32))
        eps128 = E(nc.sbuf_tensor("eps128", [128, 1], F32))
        srow0 = E(nc.sbuf_tensor("srow0", [1, 512], F32))
        srow1 = E(nc.sbuf_tensor("srow1", [1, 512], F32))
        srow2 = E(nc.sbuf_tensor("srow2", [1, 512], F32))
        nr32 = E(nc.sbuf_tensor("nr32", [2, 512], F32))
        slot32 = E(nc.sbuf_tensor("slot32", [16, 512], F32))
        base16 = E(nc.sbuf_tensor("base16", [16, 1], F32))
        stash = E(nc.sbuf_tensor("stash", [128, 256], F32))
        ebc = E(nc.sbuf_tensor("ebc", [128, 16], F32))
        trashC = E(nc.sbuf_tensor("trashC", [128, 1], F32))
        idxb = E(nc.sbuf_tensor("idxb", [128, 8], mybir.dt.int32))
        offb = E(nc.sbuf_tensor("offb", [128, 8], F32))
        tkid = E(nc.sbuf_tensor("tkid", [128, 8], mybir.dt.int32))
        itile = E(nc.sbuf_tensor("itile", [128, 65], mybir.dt.int32))
        xf_sb = E(nc.sbuf_tensor("xf_sb", [128, 2 * 512], F32))
        y_sb = E(nc.sbuf_tensor("y_sb", [128, 2 * 512], F32))

        PS = [E(nc.psum_tensor(f"P{i}", [128, 512], F32)) for i in range(8)]

        sems = {}
        for e in ENGS:
            sems[e] = E(nc.semaphore(f"sem_{e}"))
        sems["coll"] = E(nc.semaphore("sem_coll"))

        block = E(nc.Block())

        WQ, WK, WV, WO = 0, 8192, 16384, 24576
        WGS, WUS, WDS, HS = 0, 4096, 8192, 16384
        WDS2 = 8192
        XG, HTMP, DTM = 20480, 28672, 30720
        XTM, XSQ, XNTM = 12352, 14400, 16448
        QS, KS, KFT, VAUG, XSB, X2 = 0, 0, 4096, 8192, 12352, 16448
        MIX = 0
        GT, UT, CB = 4096, 6144, 8192

        def bigv(off, n):
            return big[:, off:off + n]

        def scrv(off, n):
            return scr[:, off:off + n]

        prog.emit("gpsimd", lambda g: g.iota(
            pid32[:, 0:1], [[1, 1]], channel_multiplier=1,
            allow_small_or_imprecise_dtypes=True))
        prog.emit("gpsimd", lambda g: g.iota(
            ident32[:, :], [[1, 128]], channel_multiplier=0,
            allow_small_or_imprecise_dtypes=True))
        prog.emit("gpsimd", lambda g: g.iota(
            hmask[:, :], [[1, 128]], channel_multiplier=0,
            allow_small_or_imprecise_dtypes=True))
        prog.emit("gpsimd", lambda g: g.memset(ones1[:, :], 1.0))
        prog.emit("gpsimd", lambda g: g.memset(base16[:, :], 0.0))
        prog.emit("gpsimd", lambda g: g.iota(
            ebc[:, :], [[512, 16]], base=-1, channel_multiplier=0,
            allow_small_or_imprecise_dtypes=True))
        prog.emit("gpsimd", lambda g: g.iota(
            itile[:, :], [[0, 65]], base=2048, channel_multiplier=1))
        t_itile = prog.emit("gpsimd", lambda g: g.memset(scrv(GT, 2048), 0.0))
        prog.emit("gpsimd", lambda g: g.memset(eps128[:, :], 1e-5))
        prog.emit("gpsimd", lambda g: g.memset(ones128[:, :], 1.0))
        prog.emit("gpsimd", lambda g: g.memset(
            scrv(VAUG, 4160).rearrange("p (tc h e) -> p tc h e", tc=4, h=16)[:, :, :, 64:65],
            1.0))
        tg_init = prog.tok("gpsimd")

        t_id = prog.emit("vector", lambda v: v.tensor_scalar(
            ident32[:, :], ident32[:, :], pid32[:, 0:1], None, ALU.is_equal),
            wait=[tg_init])
        t_hm1 = prog.emit("vector", lambda v: v.tensor_scalar(
            hmask[:, :], hmask[:, :], 64.0, None, ALU.is_ge))
        t_hm = prog.emit("vector", lambda v: v.tensor_scalar(
            hmask[:, :], hmask[:, :], pid32[0:2, 0:1], None, ALU.is_equal),
            wait=[t_hm1])
        t_trC = prog.emit("vector", lambda v: v.tensor_scalar(
            trashC[:, :], pid32[:, 0:1], 2048.0, None, ALU.add),
            wait=[tg_init])
        t_tok0 = prog.emit("sync", lambda g: dma(
            g, tok_hbm.ap().rearrange("(p f) -> p f", p=128), itile[:, :]),
            wait=[t_itile], inc=16)
        t_rt0 = None
        for r in range(34):
            t_rt0 = prog.emit("sync", (lambda r: lambda g: dma(
                g, rt_hbm[r * 128:(r + 1) * 128, :],
                scrv(GT, 1024)))(r), wait=[t_itile], inc=16)

        def dma(g, out, in_):
            return g.dma_start(out=out, in_=in_)

        t_bf8 = None
        for i in range(8):
            t_bf8 = prog.emit("gpsimd", (lambda i: lambda g: dma(
                g, bf8[0:1, i * 64:(i + 1) * 64], bf_d[None, :]))(i), inc=16)
        prog.emit("sync", lambda g: dma(g, bf128[0:64, 0:1], bf_d[:, None]), inc=16)
        t_bf128 = prog.emit("sync", lambda g: dma(g, bf128[64:128, 0:1], bf_d[:, None]),
                            inc=16)
        t_bo = prog.emit("sync", lambda g: g.dma_start(
            out=bo_sb[:, :], in_=bo_d.rearrange("(m p) -> p m", p=128),
            allow_slow_non_contiguous=True), inc=16)
        prog.emit("gpsimd", lambda g: g.memset(wfblk[:, :], 0.0))
        prog.emit("gpsimd", lambda g: dma(g, wfblk[0:64, 0:64], wf_d[:, :]), inc=16)
        prog.emit("gpsimd", lambda g: dma(g, wfblk[64:128, 64:128], wf_d[:, :]), inc=16)
        prog.emit("gpsimd", lambda g: dma(g, wf2[0:64, :], wf_d[:, :]), inc=16)
        t_wf2 = prog.emit("gpsimd", lambda g: dma(g, wf2[64:128, :], wf_d[:, :]), inc=16)
        t_rw = prog.emit("gpsimd", lambda g: dma(
            g, rw_s[:, :].rearrange("p (k e) -> p k e", k=DC),
            rw_d.rearrange("(k p) e -> p k e", p=128)), inc=16)

        def w_load(dst_off, src):
            tk = None
            for k in range(DC):
                tk = prog.emit("gpsimd", (lambda k: lambda g: dma(
                    g, bigv(dst_off + k * 1024, 1024), src[k * 128:(k + 1) * 128, :]))(k),
                    inc=16)
            return tk

        t_x0pre = prog.emit("gpsimd", lambda g: dma(
            g, scrv(XSB, 4096).rearrange("p (c t) -> p c t", c=DC),
            xd[0].rearrange("(c p) t -> p c t", p=128)), inc=16)
        t_wq = w_load(WQ, wq_d)
        t_wk = w_load(WK, wk_d)
        t_wv = w_load(WV, wv_d)
        t_wo = w_load(WO, wo_d)

        LVL = {"init": 0, "A1": 1, "A2": 1, "A3": 1, "A4": 1, "A": 1,
               "coll": 2, "B": 3, "F1": 4, "F2": 4, "F3": 4, "F4": 4,
               "F3a": 4, "F3b": 4, "F3c": 4, "F3d": 4,
               "ffn": 4, "fin": 5}[upto]
        FLVL = {"F1": 1, "F2": 2, "F3": 3, "F4": 4,
                "F3a": 3, "F3b": 3, "F3c": 3, "F3d": 3}.get(upto, 5)
        ISUB = {"F3a": 1, "F3b": 2, "F3c": 3, "F3d": 4}.get(upto, 5)
        ALVL = {"A1": 1, "A2": 2, "A3": 3, "A4": 4}.get(upto, 5)
        bank_free = [None] * 8
        guardA = {}
        t_x2 = [None] * NSUB
        t_stats = [None] * NSUB
        t_xload = [None] * NSUB
        t_xn = [None] * NSUB
        t_qf = [None] * DC
        t_kv_last = None
        prev_rbmm = None
        prev_mumm = None

        RC_tok = {}
        RC_bank = {}
        t_cfmQ = [None] * NSUB

        def emit_rc_a(Q, bank=7):
            RC_bank[Q] = bank
            def rmm(Q=Q):
                def f(te):
                    last = None
                    for k in range(DC):
                        last = te.matmul(PS[bank][0:16, :], rw_s[:, k * 16:(k + 1) * 16],
                                         xn[:, k * TC + Q * TS: k * TC + Q * TS + TS],
                                         start=(k == 0), stop=(k == DC - 1))
                    return last
                return f
            t_r = prog.emit("tensor", rmm(),
                            wait=wl(t_rw if Q == 0 else None, bank_free[bank]))
            t_lt = prog.emit("scalar", lambda sc: sc.copy(out=lt32[:, :],
                                                          in_=PS[bank][0:16, :]),
                             wait=[t_r])

            def trl():
                def f(te):
                    last = None
                    for tc in range(4):
                        last = te.transpose(PS[bank][:, tc * 16:(tc + 1) * 16],
                                            lt32[0:16, tc * 128:(tc + 1) * 128],
                                            ident32[0:16, 0:16])
                    return last
                return f
            t_tr = prog.emit("tensor", trl(), wait=wl(t_lt, t_id if Q == 0 else None))
            t_cmb = [None] * 4
            for tc in range(4):
                def cmb_a(tc=tc):
                    def f(v):
                        v.reduce_max(cst[:, 0:1], PS[bank][:, tc * 16:(tc + 1) * 16],
                                     axis=AX)
                        v.drain()
                        v.tensor_scalar(cst[:, 16:32],
                                        PS[bank][:, tc * 16:(tc + 1) * 16],
                                        cst[:, 0:1], None, ALU.subtract)
                        v.drain()
                        v.tensor_scalar(cst[:, 32:48], cst[:, 16:32], 0.0, None, ALU.is_ge)
                        v.drain()
                        v.scalar_tensor_tensor(cst[:, 48:64], cst[:, 32:48], -1e30,
                                               cst[:, 16:32], ALU.mult, ALU.add)
                        v.drain()
                        return v.reduce_max(cst[:, 1:2], cst[:, 48:64], axis=AX)
                    return f
                t_a = prog.emit("vector", cmb_a(), wait=[t_tr])
                t_e = prog.emit("scalar", lambda sc: sc.activation(
                    out=cst[:, 96:112], in_=cst[:, 16:32], func=AF.Exp), wait=[t_a])
                def cmb_b(tc=tc):
                    def f(v):
                        v.drain()
                        v.tensor_scalar(cst[:, 64:80], cst[:, 16:32], cst[:, 1:2], None,
                                        ALU.is_ge)
                        v.drain()
                        v.tensor_tensor(cst[:, 80:96], cst[:, 96:112], cst[:, 64:80],
                                        ALU.mult)
                        v.drain()
                        v.reduce_sum(cst[:, 2:3], cst[:, 80:96], axis=AX)
                        v.drain()
                        v.reciprocal(cst[:, 3:4], cst[:, 2:3])
                        v.drain()
                        return v.tensor_scalar(comb32[:, tc * 16:(tc + 1) * 16],
                                               cst[:, 80:96], cst[:, 3:4], None, ALU.mult)
                    return f
                t_cmb[tc] = prog.emit("vector", cmb_b(), wait=[t_e])
                def stcp(Q=Q, tc=tc):
                    def f(v):
                        v.drain()
                        return v.tensor_copy(
                            stash[:, (Q * 4 + tc) * 16:(Q * 4 + tc) * 16 + 16],
                            comb32[:, tc * 16:(tc + 1) * 16])
                    return f
                t_cmb[tc] = prog.emit("vector", stcp())
            RC_tok[Q] = t_cmb

        def emit_rc_b(Q):
            t_cmb = RC_tok[Q]
            bank = RC_bank[Q]

            def trc():
                def f(te):
                    last = None
                    for tc in range(4):
                        last = te.transpose(PS[bank][0:16, tc * 128:(tc + 1) * 128],
                                            comb32[:, tc * 16:(tc + 1) * 16],
                                            ident32[:, :])
                    return last
                return f
            t_trc = prog.emit("tensor", trc(), wait=wl(*t_cmb))
            t_cfmQ[Q] = prog.emit("scalar", (lambda Q: lambda sc: sc.copy(
                out=cfm[0:16, Q * TS:(Q + 1) * TS], in_=PS[bank][0:16, :]))(Q),
                wait=wl(t_trc))
            bank_free[bank] = t_cfmQ[Q]

        def XS(s):
            return XSB + (s % 2) * 4096

        def X2S(s):
            return XSB + ((s + 1) % 2) * 4096

        LN_tok = {}

        def emit_LN_a(s):
            nonlocal prev_rbmm, prev_mumm
            if s == 0:
                t_xload[0] = t_x0pre
            else:
                t_xload[s] = prog.emit("gpsimd", (lambda s: lambda g: dma(
                    g, scrv(XS(s), 4096).rearrange("p (c t) -> p c t", c=DC),
                    xd[s].rearrange("(c p) t -> p c t", p=128)))(s),
                    wait=wl(t_stats[s - 1]), inc=16)
            t_x2[s] = prog.emit("scalar", (lambda s: lambda sc: sc.activation(
                out=scrv(X2S(s), 4096), in_=scrv(XS(s), 4096), func=AF.Square))(s),
                wait=wl(t_xload[s], t_xn[s - 1] if s > 0 else None))

            def stats_fn(s=s):
                def f(te):
                    last = None
                    for c in range(DC):
                        last = te.matmul(PS[5][0:1, :], ones128[:, 0:1],
                                         scrv(XS(s) + c * 512, 512),
                                         start=(c == 0), stop=(c == DC - 1))
                    for c in range(DC):
                        last = te.matmul(PS[4][0:1, :], ones128[:, 0:1],
                                         scrv(X2S(s) + c * 512, 512),
                                         start=(c == 0), stop=(c == DC - 1))
                    return last
                return f
            t_stats[s] = prog.emit("tensor", stats_fn(),
                                   wait=wl(t_x2[s], bank_free[5], bank_free[4]))
            t_p5free = prog.emit("vector", lambda v: v.tensor_scalar(
                srow0[0:1, :], PS[5][0:1, :], 1.0 / D, None, ALU.mult),
                wait=[t_stats[s]])
            bank_free[5] = t_p5free
            t_p4free = prog.emit("vector", lambda v: v.tensor_scalar(
                srow1[0:1, :], PS[4][0:1, :], 1.0 / D, None, ALU.mult))
            prog.emit("vector", lambda v: v.tensor_tensor(
                srow2[0:1, :], srow0[0:1, :], srow0[0:1, :], ALU.mult))
            t_var = prog.emit("vector", lambda v: v.tensor_tensor(
                srow1[0:1, :], srow1[0:1, :], srow2[0:1, :], ALU.subtract))
            prog.emit("scalar", lambda sc: sc.activation(
                out=srow1[0:1, :], in_=srow1[0:1, :], func=AF.Ln,
                bias=eps128[0:1, 0:1]),
                wait=wl(t_var, tg_init if s == 0 else None))
            t_rstd = prog.emit("scalar", lambda sc: sc.activation(
                out=srow2[0:1, :], in_=srow1[0:1, :], func=AF.Exp, scale=-0.5))
            prog.emit("vector", lambda v: v.tensor_tensor(
                srow1[0:1, :], srow0[0:1, :], srow2[0:1, :], ALU.mult),
                wait=[t_rstd])
            prog.emit("vector", lambda v: v.tensor_copy(rowbf[0:1, 0:512], srow2[0:1, :]),
                      wait=wl(prev_rbmm))
            t_rowbf = prog.emit("vector", lambda v: v.tensor_copy(
                rowbf[0:1, 512:1024], srow1[0:1, :]), wait=wl(prev_mumm))
            LN_tok[s] = (t_p4free, t_rowbf)

        def emit_LN_b(s):
            nonlocal prev_rbmm, prev_mumm
            t_p4free, t_rowbf = LN_tok[s]
            t_rbmm = prog.emit("tensor", lambda te: te.matmul(
                PS[4][:, :], ones1[0:1, :], rowbf[0:1, 0:512], start=True, stop=True),
                wait=wl(t_rowbf, t_p4free, bank_free[4]))
            t_mumm = prog.emit("tensor", lambda te: te.matmul(
                PS[5][:, :], ones1[0:1, :], rowbf[0:1, 512:1024], start=True, stop=True),
                wait=wl(bank_free[5]))
            prev_rbmm, prev_mumm = t_rbmm, t_mumm
            t_rbev = prog.emit("scalar", lambda sc: sc.copy(out=rb_sb[:, :],
                                                            in_=PS[4][:, :]),
                               wait=[t_rbmm])
            bank_free[4] = t_rbev
            t_muev = prog.emit("scalar", lambda sc: sc.copy(out=murb_sb[:, :],
                                                            in_=PS[5][:, :]),
                               wait=[t_mumm])
            bank_free[5] = t_muev
            for c in range(DC):
                prog.emit("vector", (lambda c, s: lambda v: v.tensor_tensor(
                    xn[:, c * TC + s * TS: c * TC + s * TS + TS],
                    scrv(XS(s) + c * 512, 512), rb_sb[:, :], ALU.mult))(c, s),
                    wait=[t_rbev] if c == 0 else ())
                t_xn[s] = prog.emit("vector", (lambda c, s: lambda v: v.tensor_tensor(
                    xn[:, c * TC + s * TS: c * TC + s * TS + TS],
                    xn[:, c * TC + s * TS: c * TC + s * TS + TS],
                    murb_sb[:, :], ALU.subtract))(c, s),
                    wait=[t_muev] if c == 0 else ())

        if LVL >= 1:
            emit_LN_a(0)
            emit_LN_b(0)
        for s in range(NSUB if LVL >= 1 else 0):
            t_qevac = [None] * DC
            for m in range(DC if ALVL >= 2 else 0):
                bk = m % 2
                def qmm(m=m, s=s, bk=bk):
                    def f(te):
                        last = None
                        for k in range(DC):
                            last = te.matmul(
                                PS[bk][:, :],
                                bigv(WQ + k * 1024 + m * 128, 128),
                                xn[:, k * TC + s * TS: k * TC + s * TS + TS],
                                start=(k == 0), stop=(k == DC - 1))
                        return last
                    return f
                t_mm = prog.emit("tensor", qmm(),
                                 wait=wl(t_xn[s], t_wq if s == 0 and m == 0 else None,
                                         bank_free[bk]))
                t_qevac[m] = prog.emit("scalar", (lambda m, bk: lambda sc: sc.copy(
                    out=scrv(QS + m * 512, 512), in_=PS[bk][:, :]))(m, bk),
                    wait=[t_mm])
                bank_free[bk] = t_qevac[m]
            for cp in range(DC if ALVL >= 2 else 0):
                bk = 2 + cp % 2
                def qfmm(cp=cp, bk=bk):
                    def f(te):
                        te.matmul(PS[bk][0:64, :], wf2[0:64, :],
                                  scr[0:64, QS + cp * 512: QS + (cp + 1) * 512],
                                  start=True, stop=True, tile_position=(0, 0))
                        return te.matmul(PS[bk][64:128, :], wf2[64:128, :],
                                         scr[64:128, QS + cp * 512: QS + (cp + 1) * 512],
                                         start=True, stop=True, tile_position=(64, 64))
                    return f
                t_mm = prog.emit("tensor", qfmm(),
                                 wait=wl(t_qevac[cp],
                                         t_wf2 if s == 0 and cp == 0 else None,
                                         bank_free[bk]))
                t_qf[cp] = prog.emit("scalar", (lambda cp, bk, s: lambda sc: sc.activation(
                    out=qfa[:, cp * TC + s * TS: cp * TC + s * TS + TS],
                    in_=PS[bk][:, :], func=AF.Relu, bias=bf128[:, 0:1]))(cp, bk, s),
                    wait=wl(t_mm, t_bf128 if s == 0 and cp == 0 else None))
                bank_free[bk] = t_qf[cp]

            if s + 1 < NSUB:
                emit_LN_a(s + 1)
            t_kevac = [None] * DC
            for m in range(DC if ALVL >= 3 else 0):
                bk = m % 2
                def kmm(m=m, s=s, bk=bk):
                    def f(te):
                        last = None
                        for k in range(DC):
                            last = te.matmul(
                                PS[bk][:, :],
                                bigv(WK + k * 1024 + m * 128, 128),
                                xn[:, k * TC + s * TS: k * TC + s * TS + TS],
                                start=(k == 0), stop=(k == DC - 1))
                        return last
                    return f
                t_mm = prog.emit("tensor", kmm(),
                                 wait=wl(t_wk if s == 0 and m == 0 else None,
                                         bank_free[bk]))
                t_kevac[m] = prog.emit("scalar", (lambda m, bk: lambda sc: sc.copy(
                    out=scrv(KS + m * 512, 512), in_=PS[bk][:, :]))(m, bk),
                    wait=[t_mm])
                bank_free[bk] = t_kevac[m]
            if s == NSUB - 1:
                guardA["k"] = t_kevac[DC - 1]
            t_kf = [None] * 4
            for tc in range(4 if ALVL >= 3 else 0):
                def kfmm(tc=tc):
                    def f(te):
                        last = None
                        for bk, p0 in ((2, 0), (3, 4)):
                            for pp in range(4):
                                pair = p0 + pp
                                te.matmul(
                                    PS[bk][:, pp * 128:(pp + 1) * 128],
                                    scrv(KS + pair * 512 + tc * 128, 128),
                                    wfblk[:, :],
                                    start=(pp == 0), stop=False)
                            last = te.matmul(PS[bk][:, :], ones1[0:1, :], bf8[0:1, :],
                                             start=False, stop=True)
                        return last
                    return f
                t_mm = prog.emit("tensor", kfmm(),
                                 wait=wl(t_kevac[7],
                                         t_bf8 if s == 0 and tc == 0 else None,
                                         bank_free[2], bank_free[3]))
                ev2 = prog.emit("scalar", (lambda tc: lambda sc: sc.activation(
                    out=scrv(KFT + tc * 1024, 512), in_=PS[2][:, :], func=AF.Relu))(tc),
                    wait=wl(t_mm, t_rt0 if s == 0 and tc == 0 else None))
                ev3 = prog.emit("scalar", (lambda tc: lambda sc: sc.activation(
                    out=scrv(KFT + tc * 1024 + 512, 512), in_=PS[3][:, :],
                    func=AF.Relu))(tc))
                bank_free[2] = ev2
                bank_free[3] = ev3
                t_kf[tc] = ev3

            if s + 1 < NSUB:
                emit_LN_b(s + 1)
            t_va = [[None, None] for _ in range(4)]
            for tc in range(4 if ALVL >= 4 else 0):
                for half in range(2):
                    bk = (tc * 2 + half) % 2
                    def vmm(tc=tc, half=half, s=s, bk=bk):
                        def f(te):
                            last = None
                            for k in range(DC):
                                last = te.matmul(
                                    PS[bk][:, :],
                                    xn[:, k * TC + s * TS + tc * 128:
                                       k * TC + s * TS + (tc + 1) * 128],
                                    bigv(WV + k * 1024 + half * 512, 512),
                                    start=(k == 0), stop=(k == DC - 1))
                            return last
                        return f
                    t_mm = prog.emit("tensor", vmm(),
                                     wait=wl(t_wv if s == 0 and tc == 0 and half == 0
                                             else None,
                                             bank_free[bk]))
                    t_va[tc][half] = prog.emit("vector",
                        (lambda tc, half, bk: lambda v: v.tensor_copy(
                            scrv(VAUG + tc * 1040, 1040)
                            .rearrange("p (h e) -> p h e", h=16)
                            [:, half * 8:(half + 1) * 8, 0:64],
                            PS[bk][:, :].rearrange("p (h d) -> p h d", h=8)))(tc, half, bk),
                        wait=[t_mm])
                    bank_free[bk] = t_va[tc][half]

            if s == NSUB - 1 and LVL >= 4:
                emit_rc_a(0, bank=5)
                emit_rc_b(0)
            for tc in range(4 if ALVL >= 5 else 0):
                def kvmm(tc=tc, s=s):
                    def f(te):
                        last = None
                        for h in range(H):
                            pair = h // 2
                            bk = 6 + pair // 4
                            col = (pair % 4) * 65
                            r0 = (h % 2) * 64
                            first = (s == 0 and tc == 0 and h in (0, 1, 8, 9))
                            last = te.matmul(
                                PS[bk][r0:r0 + 64, col:col + 65],
                                scrv(KFT + tc * 1024 + h * 64, 64),
                                scrv(VAUG + tc * 1040 + h * 65, 65),
                                start=first, stop=False,
                                tile_position=(0, r0))
                        return last
                    return f
                t_kv_last = prog.emit("tensor", kvmm(),
                                      wait=wl(t_kf[tc], t_va[tc][0], t_va[tc][1]))

        if LVL < 2:
            return _finish(nc, prog, block, sems, debug)
        prog.emit("vector", lambda v: v.tensor_copy(kvsb[:, 0:260], PS[6][:, 0:260]),
                  wait=[t_kv_last])
        t_kvev = prog.emit("vector", lambda v: v.tensor_copy(kvsb[:, 260:520],
                                                             PS[7][:, 0:260]))
        bank_free[6] = t_kvev
        bank_free[7] = t_kvev
        t_ccin = prog.emit("sync", lambda g: dma(g, cc_in[:, :], kvsb[:, :]),
                           wait=[t_kvev], inc=16)
        if debug:
            prog.emit("gpsimd", lambda g: dma(g, dbg["kvp"][:, :], kvsb[:, :]),
                      wait=[t_kvev], inc=16)
        t_coll = prog.emit_coll(lambda g: g.collective_compute(
            "AllReduce", ALU.add,
            replica_groups=[[0, 1], [2, 3], [4, 5], [6, 7]],
            ins=[cc_in.ap()],
            outs=[cc_out.ap()]),
            wait=[t_ccin])
        t_kvsum = prog.emit("sync", lambda g: dma(g, kvsb[:, :], cc_out[:, :]),
                            wait=[t_coll], inc=16)
        t_kvbf = None
        t_ks2 = None
        t_dbgqf = None

        def emit_kv_post():
            nonlocal t_kvbf, t_ks2, t_dbgqf
            t_kvbf = prog.emit("vector", lambda v: v.tensor_copy(kvbf[:, :], kvsb[:, :]),
                               wait=[t_kvsum])
            if debug:
                prog.emit("gpsimd", lambda g: dma(g, dbg["kvs"][:, :], kvsb[:, :]),
                          wait=[t_kvbf], inc=16)
                for c in range(DC):
                    prog.emit("gpsimd", (lambda c: lambda g: dma(
                        g, dbg["xn"][c], xn[:, c * TC:(c + 1) * TC]))(c),
                        wait=[t_xn[3]], inc=16)
                    t_dbgqf = prog.emit("gpsimd", (lambda c: lambda g: dma(
                        g, dbg["qf"][c], qfa[:, c * TC:(c + 1) * TC]))(c),
                        wait=[t_qf[c]], inc=16)
            prog.emit("vector", lambda v: v.memset(ks2[:, :], 0.0), wait=[t_kvbf])
            for c in range(DC):
                prog.emit("vector", (lambda c: lambda v: v.tensor_copy(
                    ks2[0:64, 2 * c:2 * c + 1], kvbf[0:64, c * 65 + 64: c * 65 + 65]))(c))
                t_ks2 = prog.emit("vector", (lambda c: lambda v: v.tensor_copy(
                    ks2[64:128, 2 * c + 1:2 * c + 2],
                    kvbf[64:128, c * 65 + 64: c * 65 + 65]))(c))

        if LVL < 3:
            if LVL >= 2:
                emit_kv_post()
            return _finish(nc, prog, block, sems, debug)
        t_accB = [[None] * DC for _ in range(NSUB)]
        DBGACC = [None]

        def emit_phase_b():
            t_wolast = None
            prev_nb = {}
            prev_bc = {}
            for s in range(NSUB):
                t_mix = [None] * DC
                for c in range(DC):
                    po = 0 if c % 2 == 0 else 5
                    pn = 1 if c % 2 == 0 else 3
                    pb = 2 if c % 2 == 0 else 4
                    par = c % 2
                    def outmm(c=c, s=s, po=po):
                        def f(te):
                            te.matmul(PS[po][0:64, :], kvbf[0:64, c * 65: c * 65 + 64],
                                      qfa[0:64, c * TC + s * TS: c * TC + s * TS + TS],
                                      start=True, stop=True, tile_position=(0, 0))
                            return te.matmul(PS[po][64:128, :],
                                             kvbf[64:128, c * 65: c * 65 + 64],
                                             qfa[64:128, c * TC + s * TS: c * TC + s * TS + TS],
                                             start=True, stop=True, tile_position=(64, 64))
                        return f
                    t_out = prog.emit("tensor", outmm(),
                                      wait=wl(t_kvbf, t_dbgqf, bank_free[po]))
                    def nmm(c=c, s=s, pn=pn):
                        return lambda te: te.matmul(
                            PS[pn][0:2, :], ks2[:, 2 * c:2 * c + 2],
                            qfa[:, c * TC + s * TS: c * TC + s * TS + TS],
                            start=True, stop=True)
                    t_nmm = prog.emit("tensor", nmm(), wait=wl(t_ks2, bank_free[pn]))
                    t_nm = prog.emit("vector", (lambda pn: lambda v: v.tensor_scalar(
                        nr32[0:2, :], PS[pn][0:2, :], 1e-6, None, ALU.max))(pn),
                        wait=[t_nmm])
                    bank_free[pn] = t_nm
                    prog.emit("scalar", lambda sc: sc.activation(
                        out=nr32[0:2, :], in_=nr32[0:2, :], func=AF.Ln),
                        wait=[t_nm])
                    t_nr = prog.emit("scalar", (lambda par: lambda sc: sc.activation(
                        out=nrbf[0:2, par * 512:(par + 1) * 512], in_=nr32[0:2, :],
                        func=AF.Exp, scale=-1.0))(par),
                        wait=wl(prev_bc.get(par)))
                    t_bc = prog.emit("tensor", (lambda par, pb: lambda te: te.matmul(
                        PS[pb][:, :], hmask[0:2, :], nrbf[0:2, par * 512:(par + 1) * 512],
                        start=True, stop=True))(par, pb),
                        wait=wl(t_nr, t_hm if s == 0 and c < 2 else None, bank_free[pb]))
                    prev_bc[par] = t_bc
                    t_nb = prog.emit("scalar", (lambda par, pb: lambda sc: sc.copy(
                        out=nbsb[:, par * 512:(par + 1) * 512], in_=PS[pb][:, :]))(par, pb),
                        wait=wl(t_bc, prev_nb.get(par)))
                    bank_free[pb] = t_nb
                    t_mix[c] = prog.emit("vector", (lambda c, par, po: lambda v: v.tensor_tensor(
                        scrv(MIX + c * 512, 512), PS[po][:, :],
                        nbsb[:, par * 512:(par + 1) * 512], ALU.mult))(c, par, po),
                        wait=wl(t_nb, t_out, t_wolast))
                    prev_nb[par] = t_mix[c]
                    bank_free[po] = t_mix[c]
                for m in range(DC):
                    bk = 1 if m % 2 == 0 else 3
                    def womm(m=m, bk=bk):
                        def f(te):
                            last = None
                            for k in range(DC):
                                last = te.matmul(PS[bk][:, :],
                                                 bigv(WO + k * 1024 + m * 128, 128),
                                                 scrv(MIX + k * 512, 512),
                                                 start=(k == 0), stop=(k == DC - 1))
                            return last
                        return f
                    t_mm = prog.emit("tensor", womm(),
                                     wait=wl(t_wo if m == 0 and s == 0 else None,
                                             t_mix[7], bank_free[bk]))
                    if m == DC - 1:
                        t_wolast = t_mm
                    t_accB[s][m] = prog.emit("scalar", (lambda m, s, bk: lambda sc: sc.activation(
                        out=qfa[:, m * TC + s * TS: m * TC + s * TS + TS],
                        in_=PS[bk][:, :], func=AF.Identity, bias=bo_sb[:, m:m + 1]))(m, s, bk),
                        wait=wl(t_mm, t_bo if m == 0 and s == 0 else None))
                    bank_free[bk] = t_accB[s][m]
            t_dbgacc = None
            if debug:
                for c in range(DC):
                    t_dbgacc = prog.emit("gpsimd", (lambda c: lambda g: dma(
                        g, dbg["acc"][c], qfa[:, c * TC:(c + 1) * TC]))(c),
                        wait=[t_accB[3][c]], inc=16)

        if LVL < 4:
            if LVL >= 3:
                emit_kv_post()
                emit_phase_b()
            return _finish(nc, prog, block, sems, debug)

        t_shev = [[None] * DC for _ in range(NSUB)]
        t_gmm = {}
        t_umm = {}
        t_dmm = {}
        t_htok = {}
        t_sguh = {}
        t_wdld = {}
        t_xn_tm = {}
        t_q_scat = [None] * NSUB
        t_cbmm_l = {}
        t_uev_l = {}
        t_scA = {}
        t_xgsc = {}

        t_guhld = {}

        def emit_guh_load(Q, j):
            n, hf = j // 2, j % 2
            par = j % 2
            wg_src = wgs_d[n][:, hf * 256:(hf + 1) * 256].rearrange(
                "(k p) m -> p k m", p=128)
            wu_src = wus_d[n][:, hf * 256:(hf + 1) * 256].rearrange(
                "(k p) m -> p k m", p=128)
            guard = guardA.get("q") if (Q == 0 and j < 2) else None
            t_wg = prog.emit("gpsimd", (lambda src=wg_src, par=par: lambda g: dma(
                g, bigv(WGS + par * 2048, 2048).rearrange("p (k m) -> p k m", k=DC),
                src))(),
                wait=wl(t_gmm.get(("s", Q, j - 2)), guard,
                        t_gmm.get(("s", Q - 1, 2 + par))), inc=16)
            t_wu = prog.emit("gpsimd", (lambda src=wu_src, par=par: lambda g: dma(
                g, bigv(WUS + par * 2048, 2048).rearrange("p (k m) -> p k m", k=DC),
                src))(),
                wait=wl(t_umm.get(("s", Q, j - 2)), guard,
                        t_umm.get(("s", Q - 1, 2 + par))), inc=16)
            t_guhld[(Q, j)] = (t_wg, t_wu)
            return t_wg, t_wu

        def emit_shared_guh(Q):
            for j in range(4):
                n, hf = j // 2, j % 2
                par = j % 2
                if (Q, j) in t_guhld:
                    t_wg, t_wu = t_guhld[(Q, j)]
                else:
                    t_wg, t_wu = emit_guh_load(Q, j)

                def gmm(Q=Q, par=par):
                    def f(te):
                        last = None
                        for mc in range(2):
                            for k in range(DC):
                                last = te.matmul(
                                    PS[mc][:, :],
                                    bigv(WGS + par * 2048 + k * 256 + mc * 128, 128),
                                    xn[:, k * TC + Q * TS: k * TC + Q * TS + TS],
                                    start=(k == 0), stop=(k == DC - 1))
                        return last
                    return f
                t_gmm[("s", Q, j)] = prog.emit("tensor", gmm(),
                                               wait=wl(t_wg, bank_free[0],
                                                       bank_free[1]))
                prog.emit("scalar", (lambda par: lambda sc: sc.activation(
                    out=scrv(GT + par * 1024, 512), in_=PS[0][:, :],
                    func=AF.Silu))(par),
                    wait=wl(t_gmm[("s", Q, j)], t_htok.get(("s", Q, j - 2)),
                            t_htok.get(("s", Q - 1, 2 + par))))
                t_gev2 = prog.emit("scalar", (lambda par: lambda sc: sc.activation(
                    out=scrv(GT + par * 1024 + 512, 512), in_=PS[1][:, :],
                    func=AF.Silu))(par))
                bank_free[0] = t_gev2
                bank_free[1] = t_gev2

                def umm(Q=Q, par=par):
                    def f(te):
                        last = None
                        for mc in range(2):
                            for k in range(DC):
                                last = te.matmul(
                                    PS[2 + mc][:, :],
                                    bigv(WUS + par * 2048 + k * 256 + mc * 128, 128),
                                    xn[:, k * TC + Q * TS: k * TC + Q * TS + TS],
                                    start=(k == 0), stop=(k == DC - 1))
                        return last
                    return f
                t_umm[("s", Q, j)] = prog.emit("tensor", umm(),
                                               wait=wl(t_wu, bank_free[2],
                                                       bank_free[3]))
                prog.emit("vector", (lambda par: lambda v: v.tensor_copy(
                    scrv(UT + par * 1024, 512), PS[2][:, :]))(par),
                    wait=[t_umm[("s", Q, j)]])
                t_u2 = prog.emit("vector", (lambda par: lambda v: v.tensor_copy(
                    scrv(UT + par * 1024 + 512, 512), PS[3][:, :]))(par))
                bank_free[2] = t_u2
                bank_free[3] = t_u2
                t_htok[("s", Q, j)] = prog.emit("vector",
                    (lambda j, par: lambda v: v.tensor_tensor(
                        bigv(HS + j * 1024, 1024), scrv(GT + par * 1024, 1024),
                        scrv(UT + par * 1024, 1024), ALU.mult))(j, par),
                    wait=wl(t_gev2, t_dmm.get(("s", Q - 1, DC - 1))))
            t_sguh[Q] = t_htok[("s", Q, 3)]

        def emit_down_load(Q):
            for j in range(4):
                n, hf = j // 2, j % 2
                wd_src = wds_d[n][hf * 256:(hf + 1) * 256, :].rearrange(
                    "(k p) m -> p k m", p=128)
                guard = guardA.get("k") if Q == 0 else None
                t_wdld[(Q, j)] = prog.emit("gpsimd",
                    (lambda src=wd_src, j=j: lambda g: dma(
                        g, bigv(WDS2 + j * 2048, 2048).rearrange(
                            "p (k m) -> p k m", k=2), src))(),
                    wait=wl(t_dmm.get(("s", Q - 1, DC - 1)), guard), inc=16)

        def emit_shared_down(Q):
            if (Q, 3) not in t_wdld:
                emit_down_load(Q)
            for m in range(DC):
                bk = m % 2
                def dmmf(Q=Q, m=m, bk=bk):
                    def f(te):
                        last = None
                        for j in range(4):
                            for kc in range(2):
                                last = te.matmul(
                                    PS[bk][:, :],
                                    bigv(WDS2 + j * 2048 + kc * 1024 + m * 128, 128),
                                    bigv(HS + j * 1024 + kc * 512, 512),
                                    start=(j == 0 and kc == 0),
                                    stop=(j == 3 and kc == 1))
                        return last
                    return f
                t_dmm[("s", Q, m)] = prog.emit("tensor", dmmf(),
                    wait=wl(t_wdld[(Q, 3)], t_sguh[Q], bank_free[bk]))
                t_shev[Q][m] = prog.emit("vector",
                    (lambda m, Q, bk: lambda v: v.tensor_tensor(
                        qfa[:, m * TC + Q * TS: m * TC + Q * TS + TS], PS[bk][:, :],
                        qfa[:, m * TC + Q * TS: m * TC + Q * TS + TS], ALU.add))(
                        m, Q, bk),
                    wait=wl(t_dmm[("s", Q, m)], t_accB[Q][m],
                            DBGACC[0] if Q == 0 else None))
                bank_free[bk] = t_shev[Q][m]

        def emit_idx_scat(Q):
            prog.emit("vector", (lambda Q: lambda v: v.tensor_scalar(
                lt32[:, :], cfm[0:16, Q * TS:(Q + 1) * TS], 0.0, None,
                ALU.is_gt))(Q),
                wait=wl(t_cfmQ[Q]))
            def scanf(v):
                v.drain()
                return v.tensor_tensor_scan(
                    slot32[:, :], lt32[:, :], lt32[:, :], 0.0, ALU.add, ALU.bypass)
            prog.emit("vector", scanf)

            def baseadd(v):
                v.drain()
                return v.tensor_scalar(slot32[:, :], slot32[:, :], base16[:, 0:1],
                                       None, ALU.add)
            prog.emit("vector", baseadd)

            def newbase(v):
                v.drain()
                return v.tensor_copy(base16[:, 0:1], slot32[:, 511:512])
            t_slots = prog.emit("vector", newbase)
            if ISUB < 2:
                return

            def tr4(te):
                last = None
                for tc in range(4):
                    last = te.transpose(PS[7][:, tc * 16:(tc + 1) * 16],
                                        slot32[0:16, tc * 128:(tc + 1) * 128],
                                        ident32[0:16, 0:16])
                return last
            t_tr = prog.emit("tensor", tr4, wait=wl(t_slots, bank_free[7]))
            t_idx_last = None
            for tc in range(4):
                i = Q * 4 + tc
                par = i % 2
                if ISUB < 3:
                    bank_free[7] = t_tr
                    continue
                t_ld = None
                for f in range(DC):
                    t_ld = prog.emit("sync", (lambda i, par, f: lambda g: g.dma_start(
                        out=scrv(XNTM + par * 1024 + f * 128, 128),
                        in_=xnh[f * 128:(f + 1) * 128, i * 128:(i + 1) * 128],
                        transpose=True))(i, par, f),
                        wait=wl(t_xnst if f == 0 else None,
                                t_xgsc.get(i - 2) if f == 0 else None), inc=16)
                prog.emit("sync", lambda g: g.nop(),
                          wait=[prog.tok("sync")], inc=0)
                t_xn_tm[i] = t_ld
                if debug and i == 0:
                    prog.emit("gpsimd", (lambda par: lambda g: dma(
                        g, dbg["xntm"][:, :], scrv(XNTM + par * 1024, 1024)))(par),
                        wait=[t_xn_tm[0]], inc=16)

                def idxops(i=i, tc=tc):
                    def f(v):
                        Ct = stash[:, i * 16:(i + 1) * 16]
                        St = PS[7][:, tc * 16:(tc + 1) * 16]
                        v.drain()
                        v.reduce_max(cst[:, 0:1], Ct, axis=AX)
                        v.drain()
                        v.tensor_scalar(cst[:, 16:32], Ct, cst[:, 0:1], None,
                                        ALU.is_equal)
                        v.drain()
                        v.tensor_tensor_scan(cst[:, 32:48], cst[:, 16:32],
                                             cst[:, 16:32], 0.0, ALU.add, ALU.bypass)
                        v.drain()
                        v.tensor_scalar(cst[:, 48:64], cst[:, 32:48], 1.0, None,
                                        ALU.is_equal)
                        v.drain()
                        v.tensor_tensor(cst[:, 16:32], cst[:, 16:32], cst[:, 48:64],
                                        ALU.mult)
                        v.drain()
                        v.tensor_scalar(cst[:, 32:48], Ct, 0.0, None, ALU.is_gt)
                        v.drain()
                        v.tensor_tensor(cst[:, 32:48], cst[:, 32:48], cst[:, 16:32],
                                        ALU.subtract)
                        v.drain()
                        v.tensor_tensor_scan(cst[:, 48:64], cst[:, 32:48],
                                             cst[:, 32:48], 0.0, ALU.add, ALU.bypass)
                        v.drain()
                        v.tensor_scalar(cst[:, 64:80], cst[:, 48:64], 1.0, None,
                                        ALU.is_equal)
                        v.drain()
                        v.tensor_tensor(cst[:, 32:48], cst[:, 32:48], cst[:, 64:80],
                                        ALU.mult)
                        v.drain()
                        v.tensor_tensor(cst[:, 48:64], St, ebc[:, :], ALU.add)
                        v.drain()
                        v.tensor_scalar(cst[:, 64:80], St, float(CAP) + 0.5, None,
                                        ALU.is_gt)
                        v.drain()
                        v.scalar_tensor_tensor(cst[:, 48:64], cst[:, 64:80], 16384.0,
                                               cst[:, 48:64], ALU.mult, ALU.add)
                        v.drain()
                        v.tensor_scalar(cst[:, 48:64], cst[:, 48:64], 8255.0, None,
                                        ALU.min)
                        v.drain()
                        v.tensor_tensor(cst[:, 64:80], cst[:, 16:32], cst[:, 48:64],
                                        ALU.mult)
                        v.drain()
                        v.reduce_sum(offb[:, 0:1], cst[:, 64:80], axis=AX)
                        v.drain()
                        v.tensor_tensor(cst[:, 64:80], cst[:, 32:48], cst[:, 48:64],
                                        ALU.mult)
                        v.drain()
                        v.reduce_sum(offb[:, 1:2], cst[:, 64:80], axis=AX)
                        v.drain()
                        v.tensor_tensor(cst[:, 64:80], cst[:, 16:32], Ct, ALU.mult)
                        v.drain()
                        v.reduce_sum(offb[:, 2:3], cst[:, 64:80], axis=AX)
                        v.drain()
                        v.tensor_tensor(cst[:, 64:80], cst[:, 32:48], Ct, ALU.mult)
                        v.drain()
                        v.reduce_sum(offb[:, 3:4], cst[:, 64:80], axis=AX)
                        v.drain()
                        v.reduce_sum(cst[:, 1:2], cst[:, 32:48], axis=AX)
                        v.drain()
                        v.tensor_scalar(cst[:, 2:3], cst[:, 1:2], 0.5, None,
                                        ALU.is_lt)
                        v.drain()
                        v.tensor_scalar(cst[:, 7:8], cst[:, 2:3], 8255.0, None,
                                        ALU.mult)
                        v.drain()
                        v.tensor_tensor(offb[:, 1:2], offb[:, 1:2], cst[:, 7:8],
                                        ALU.add)
                        v.drain()
                        v.tensor_scalar(offb[:, 4:5], pid32[:, 0:1], float(i * 128),
                                        None, ALU.add)
                        v.drain()
                        v.tensor_tensor(cst[:, 3:4], trashC[:, 0:1], offb[:, 4:5],
                                        ALU.subtract)
                        v.drain()
                        v.tensor_scalar(cst[:, 2:3], offb[:, 0:1], 8191.5, None,
                                        ALU.is_gt)
                        v.drain()
                        v.tensor_tensor(cst[:, 7:8], cst[:, 2:3], cst[:, 3:4],
                                        ALU.mult)
                        v.drain()
                        v.tensor_tensor(offb[:, 5:6], cst[:, 7:8], offb[:, 4:5],
                                        ALU.add)
                        v.drain()
                        v.tensor_scalar(cst[:, 2:3], offb[:, 1:2], 8191.5, None,
                                        ALU.is_gt)
                        v.drain()
                        v.tensor_tensor(cst[:, 7:8], cst[:, 2:3], cst[:, 3:4],
                                        ALU.mult)
                        v.drain()
                        v.tensor_tensor(offb[:, 6:7], cst[:, 7:8], offb[:, 4:5],
                                        ALU.add)
                        v.drain()
                        v.tensor_scalar(offb[:, 6:7], offb[:, 6:7], float(TC + 128),
                                        None, ALU.add)
                        v.drain()
                        v.tensor_copy(idxb[:, 0:1], offb[:, 0:1])
                        v.drain()
                        v.tensor_copy(idxb[:, 1:2], offb[:, 1:2])
                        v.drain()
                        v.tensor_copy(tkid[:, 0:1], offb[:, 5:6])
                        v.drain()
                        return v.tensor_copy(tkid[:, 1:2], offb[:, 6:7])
                    return f
                if ISUB < 4:
                    bank_free[7] = t_tr
                    continue
                t_idx = prog.emit("vector", idxops(),
                                  wait=wl(t_tr, t_trC, t_scA.get(("i", i - 1))))
                t_idx_last = t_idx
                if debug and Q == 0:
                    prog.emit("gpsimd", (lambda i: lambda g: dma(
                        g, dbg["offb"][i], offb[:, :]))(i), wait=[t_idx], inc=16)
                if debug and i == 0:
                    prog.emit("gpsimd", lambda g: dma(
                        g, dbg["stash"][:, :], stash[:, :]), wait=[t_idx], inc=16)
                    prog.emit("gpsimd", lambda g: dma(
                        g, dbg["cstF"][:, :], cst[:, :]), wait=[t_idx], inc=16)
                if debug and Q == 0 and tc == 3:
                    prog.emit("gpsimd", lambda g: dma(
                        g, dbg["slots"][:, :], slot32[:, :]), wait=[t_idx], inc=16)
                if ISUB < 5:
                    continue
                prog.emit("gpsimd", (lambda par: lambda g: g.indirect_dma_start(
                    out=xg_hbm.ap(),
                    out_offset=IndirectOffsetOnAxis(ap=idxb[:, 0:1], axis=0),
                    in_=scrv(XNTM + par * 1024, 1024), in_offset=None))(par),
                    wait=wl(t_idx, t_xn_tm[i]), inc=16)
                t_xgsc[i] = prog.emit("gpsimd",
                    (lambda par: lambda g: g.indirect_dma_start(
                        out=xg_hbm.ap(),
                        out_offset=IndirectOffsetOnAxis(ap=idxb[:, 1:2], axis=0),
                        in_=scrv(XNTM + par * 1024, 1024), in_offset=None))(par),
                    inc=16)
                prog.emit("gpsimd", lambda g: g.indirect_dma_start(
                    out=tok_hbm.ap()[:, None],
                    out_offset=IndirectOffsetOnAxis(ap=idxb[:, 0:1], axis=0),
                    in_=tkid[:, 0:1], in_offset=None),
                    wait=[t_tok0], inc=16)
                prog.emit("gpsimd", lambda g: g.indirect_dma_start(
                    out=tok_hbm.ap()[:, None],
                    out_offset=IndirectOffsetOnAxis(ap=idxb[:, 1:2], axis=0),
                    in_=tkid[:, 1:2], in_offset=None),
                    inc=16)
                prog.emit("gpsimd", lambda g: g.indirect_dma_start(
                    out=cg_hbm.ap()[:, None],
                    out_offset=IndirectOffsetOnAxis(ap=idxb[:, 0:1], axis=0),
                    in_=offb[:, 2:3], in_offset=None),
                    inc=16)
                t_scA[("i", i)] = prog.emit("gpsimd",
                    lambda g: g.indirect_dma_start(
                        out=cg_hbm.ap()[:, None],
                        out_offset=IndirectOffsetOnAxis(ap=idxb[:, 1:2], axis=0),
                        in_=offb[:, 3:4], in_offset=None),
                    inc=16)
            if t_idx_last is not None:
                bank_free[7] = t_idx_last
            t_q_scat[Q] = t_scA.get(("i", Q * 4 + 3))

        t_sprld = {}

        def emit_sparse_load(e):
            par = e % 2
            wg_src = wgr_d[e].rearrange("(k p) m -> p k m", p=128)
            wu_src = wur_d[e].rearrange("(k p) m -> p k m", p=128)
            wd_src = wdr_d[e].rearrange("(k p) m -> p k m", p=128)
            t_wg = prog.emit("gpsimd", (lambda src=wg_src, par=par: lambda g: dma(
                g, bigv(WGS + par * 2048, 2048).rearrange("p (k m) -> p k m",
                                                          k=DC), src))(),
                wait=wl(t_gmm.get(("r", e - 2)), t_gmm.get(("s", 3, 2 + par))),
                inc=16)
            t_wu = prog.emit("gpsimd", (lambda src=wu_src, par=par: lambda g: dma(
                g, bigv(WUS + par * 2048, 2048).rearrange("p (k m) -> p k m",
                                                          k=DC), src))(),
                wait=wl(t_umm.get(("r", e - 2)), t_umm.get(("s", 3, 2 + par))),
                inc=16)
            t_wd = prog.emit("gpsimd", (lambda src=wd_src, par=par: lambda g: dma(
                g, bigv(WDS + par * 2048, 2048).rearrange("p (k m) -> p k m",
                                                          k=2), src))(),
                wait=wl(t_dmm.get(("r", e - 2)),
                        t_dmm.get(("s", 3, DC - 1)) if e < 2 else None), inc=16)
            t_sprld[e] = (t_wg, t_wu, t_wd)
            return t_wg, t_wu, t_wd

        def emit_sparse():
            for e in range(NE):
                par = e % 2
                if e in t_sprld:
                    t_wg, t_wu, t_wd = t_sprld[e]
                else:
                    t_wg, t_wu, t_wd = emit_sparse_load(e)
                t_cg = prog.emit("gpsimd", (lambda e: lambda g: dma(
                    g, bf8[0:1, :], cg_hbm[e * CAP:(e + 1) * CAP][None, :]))(e),
                    wait=wl(t_q_scat[3], t_cbmm_l.get(e - 1)), inc=16)
                t_xg = None
                for k in range(DC):
                    t_xg = prog.emit("sync", (lambda e, k, par: lambda g: g.dma_start(
                        out=bigv(XG + par * 4096 + k * 512, 512),
                        in_=xg_hbm[e * CAP:(e + 1) * CAP, k * 128:(k + 1) * 128],
                        transpose=True))(e, k, par),
                        wait=wl(t_q_scat[3] if k == 0 else None,
                                t_gmm.get(("r", e - 2)) if k == 0 else None,
                                t_umm.get(("r", e - 2)) if k == 0 else None),
                        inc=16)
                t_xga = t_xg
                t_xgb = None
                t_tk = None
                for sc_i in range(4):
                    t_tk = prog.emit("sync", (lambda e, par, sc_i: lambda g: dma(
                        g, tkid[:, par * 4 + sc_i: par * 4 + sc_i + 1],
                        tok_hbm[e * CAP + sc_i * 128: e * CAP + (sc_i + 1) * 128]
                        [:, None]))(e, par, sc_i),
                        wait=wl(t_q_scat[3] if sc_i == 0 else None,
                                t_scA.get(e - 2) if sc_i == 0 else None), inc=16)
                prog.emit("sync", lambda g: g.nop(),
                          wait=[prog.tok("sync")], inc=0)
                t_cbmm = prog.emit("tensor", lambda te: te.matmul(
                    PS[4][:, :], ones1[0:1, :], bf8[0:1, :], start=True, stop=True),
                    wait=wl(t_cg, bank_free[4]))
                t_cbmm_l[e] = t_cbmm
                t_cbev = prog.emit("scalar", (lambda par: lambda sc: sc.copy(
                    out=scrv(CB + par * 512, 512), in_=PS[4][:, :]))(par),
                    wait=wl(t_cbmm, t_uev_l.get(e - 2)))
                bank_free[4] = t_cbev

                def gmmr(par=par):
                    def f(te):
                        last = None
                        for mc in range(2):
                            for k in range(DC):
                                last = te.matmul(
                                    PS[mc][:, :],
                                    bigv(WGS + par * 2048 + k * 256 + mc * 128, 128),
                                    bigv(XG + par * 4096 + k * 512, 512),
                                    start=(k == 0), stop=(k == DC - 1))
                        return last
                    return f
                t_gmm[("r", e)] = prog.emit("tensor", gmmr(),
                                            wait=wl(t_wg, t_xga, t_xgb, bank_free[0],
                                                    bank_free[1]))
                prog.emit("scalar", (lambda par: lambda sc: sc.activation(
                    out=scrv(GT + par * 1024, 512), in_=PS[0][:, :],
                    func=AF.Silu))(par),
                    wait=wl(t_gmm[("r", e)], t_htok.get(("r", e - 2))))
                t_gev2 = prog.emit("scalar", (lambda par: lambda sc: sc.activation(
                    out=scrv(GT + par * 1024 + 512, 512), in_=PS[1][:, :],
                    func=AF.Silu))(par))
                bank_free[0] = t_gev2
                bank_free[1] = t_gev2

                def ummr(par=par):
                    def f(te):
                        last = None
                        for mc in range(2):
                            for k in range(DC):
                                last = te.matmul(
                                    PS[2 + mc][:, :],
                                    bigv(WUS + par * 2048 + k * 256 + mc * 128, 128),
                                    bigv(XG + par * 4096 + k * 512, 512),
                                    start=(k == 0), stop=(k == DC - 1))
                        return last
                    return f
                t_umm[("r", e)] = prog.emit("tensor", ummr(),
                                            wait=wl(t_wu, bank_free[2],
                                                    bank_free[3]))
                prog.emit("vector", (lambda par: lambda v: v.tensor_tensor(
                    scrv(UT + par * 1024, 512), PS[2][:, :],
                    scrv(CB + par * 512, 512), ALU.mult))(par),
                    wait=wl(t_umm[("r", e)], t_cbev))
                t_u2 = prog.emit("vector", (lambda par: lambda v: v.tensor_tensor(
                    scrv(UT + par * 1024 + 512, 512), PS[3][:, :],
                    scrv(CB + par * 512, 512), ALU.mult))(par))
                t_uev_l[e] = t_u2
                bank_free[2] = t_u2
                bank_free[3] = t_u2
                t_htok[("r", e)] = prog.emit("vector",
                    (lambda par: lambda v: v.tensor_tensor(
                        bigv(HTMP + par * 1024, 1024), scrv(GT + par * 1024, 1024),
                        scrv(UT + par * 1024, 1024), ALU.mult))(par),
                    wait=wl(t_gev2, t_dmm.get(("r", e - 2))))
                for sc_i in range(4):
                    evs = []
                    for nh in range(2):
                        bk = 5 + (sc_i * 2 + nh) % 2
                        def dmmr(par=par, sc_i=sc_i, nh=nh, bk=bk):
                            def f(te):
                                last = None
                                for kc in range(2):
                                    last = te.matmul(
                                        PS[bk][:, :],
                                        bigv(HTMP + par * 1024 + kc * 512
                                             + sc_i * 128, 128),
                                        bigv(WDS + par * 2048 + kc * 1024
                                             + nh * 512, 512),
                                        start=(kc == 0), stop=(kc == 1))
                                return last
                            return f
                        t_d = prog.emit("tensor", dmmr(),
                            wait=wl(t_wd, t_htok[("r", e)], bank_free[bk]))
                        ev = prog.emit("scalar",
                            (lambda sc_i, nh, bk: lambda sc: sc.copy(
                                out=bigv(DTM + (sc_i % 2) * 1024 + nh * 512, 512),
                                in_=PS[bk][:, :]))(sc_i, nh, bk),
                            wait=wl(t_d, t_scA.get((e, sc_i - 2))))
                        bank_free[bk] = ev
                        evs.append(ev)
                    t_scA[(e, sc_i)] = prog.emit("gpsimd",
                        (lambda sc_i, par: lambda g: g.indirect_dma_start(
                            out=rt_hbm.ap(),
                            out_offset=IndirectOffsetOnAxis(
                                ap=tkid[:, par * 4 + sc_i: par * 4 + sc_i + 1],
                                axis=0),
                            in_=bigv(DTM + (sc_i % 2) * 1024, 1024),
                            in_offset=None))(sc_i, par),
                        wait=wl(evs[0], evs[1], t_tk, t_rt0), inc=16)
                t_scA[e] = t_scA[(e, 3)]
                t_dmm[("r", e)] = t_scA[(e, 3)]
            return t_scA[(NE - 1, 3)]

        t_sqprev = [None]
        emit_rc_a(1, bank=7)
        emit_rc_b(1)
        emit_rc_a(2, bank=7)
        emit_rc_b(2)
        emit_rc_a(3, bank=7)
        emit_rc_b(3)
        if FLVL < 2:
            emit_kv_post()
            emit_phase_b()
            return _finish(nc, prog, block, sems, debug)
        t_xnst = None
        for c in range(DC):
            t_xnst = prog.emit("sync", (lambda c: lambda g: dma(
                g, xnh[c * 128:(c + 1) * 128, :], xn[:, c * TC:(c + 1) * TC]))(c),
                wait=wl(t_xn[3] if c == 0 else None), inc=16)
        emit_shared_guh(0)
        emit_kv_post()
        emit_phase_b()
        emit_shared_down(0)
        if FLVL < 3:
            return _finish(nc, prog, block, sems, debug)
        if FLVL < 4:
            emit_idx_scat(0)
            return _finish(nc, prog, block, sems, debug)
        emit_shared_guh(1)
        emit_shared_down(1)
        emit_guh_load(2, 0)
        emit_guh_load(2, 1)
        emit_down_load(2)
        emit_idx_scat(0)
        emit_shared_guh(2)
        emit_shared_down(2)
        emit_guh_load(3, 0)
        emit_guh_load(3, 1)
        emit_down_load(3)
        emit_idx_scat(1)
        emit_shared_guh(3)
        emit_shared_down(3)
        emit_sparse_load(0)
        emit_sparse_load(1)
        emit_idx_scat(2)
        emit_idx_scat(3)
        if FLVL < 5:
            return _finish(nc, prog, block, sems, debug)
        t_allscat = emit_sparse()
        if debug:
            for r in range(2):
                prog.emit("gpsimd", (lambda r: lambda g: dma(
                    g, dbg["rt"][r * 128:(r + 1) * 128, :],
                    rt_hbm[r * 128:(r + 1) * 128, :]))(r),
                    wait=[t_allscat], inc=16)
            prog.emit("gpsimd", lambda g: dma(
                g, dbg["xg"][:, :], xg_hbm[8 * CAP:8 * CAP + 128, :]),
                wait=[t_allscat], inc=16)
            prog.emit("gpsimd", lambda g: dma(
                g, dbg["cgh"][:, None], cg_hbm[8 * CAP:9 * CAP][:, None]),
                wait=[t_allscat], inc=16)
            prog.emit("gpsimd", lambda g: dma(
                g, dbg["tokh"][:, None], tok_hbm[8 * CAP:9 * CAP][:, None]),
                wait=[t_allscat], inc=16)
            prog.emit("gpsimd", lambda g: dma(g, dbg["comb"][:, :],
                                              cfm[0:16, 0:TS]),
                      wait=[t_cfmQ[0]], inc=16)
            for c in range(DC):
                prog.emit("gpsimd", (lambda c: lambda g: dma(
                    g, dbg["ffn"][c], qfa[:, c * TC:(c + 1) * TC]))(c),
                    wait=[t_shev[3][c]], inc=16)

        if LVL < 5:
            return _finish(nc, prog, block, sems, debug)
        t_ytt = {}
        t_yst = {}
        for si in range(NSUB):
            for m in range(DC):
                idx = si * DC + m
                par = idx % 2
                t_xf = prog.emit("sync", (lambda si, m, par: lambda g: dma(
                    g, xf_sb[:, par * 512:(par + 1) * 512],
                    xd[si][m * 128:(m + 1) * 128, :]))(si, m, par),
                    wait=wl(t_ytt.get(idx - 2)), inc=16)
                t_rf1 = prog.emit("sync", (lambda si, m, par: lambda g: g.dma_start(
                    out=scrv(XTM + par * 512, 512),
                    in_=rt_hbm[si * 512:(si + 1) * 512, m * 128:(m + 1) * 128],
                    transpose=True))(si, m, par),
                    wait=wl(t_allscat, t_ytt.get(idx - 2)), inc=16)
                prog.emit("sync", lambda g: g.nop(),
                          wait=[prog.tok("sync")], inc=0)
                t_rf2 = prog.emit("scalar", (lambda si, m, par: lambda g: g.dma_start(
                    out=scrv(XNTM + par * 512, 512),
                    in_=rt_hbm[TC + 128 + si * 512: TC + 128 + (si + 1) * 512,
                               m * 128:(m + 1) * 128],
                    transpose=True))(si, m, par),
                    wait=wl(t_allscat, t_ytt.get(idx - 2)), inc=16)
                prog.emit("scalar", lambda sc: sc.nop(),
                          wait=[prog.tok("scalar")], inc=0)

                def ytt(si=si, m=m, par=par):
                    def f(v):
                        v.tensor_tensor(
                            y_sb[:, par * 512:(par + 1) * 512],
                            qfa[:, m * TC + si * TS: m * TC + si * TS + TS],
                            xf_sb[:, par * 512:(par + 1) * 512], ALU.add)
                        v.tensor_tensor(
                            y_sb[:, par * 512:(par + 1) * 512],
                            y_sb[:, par * 512:(par + 1) * 512],
                            scrv(XTM + par * 512, 512), ALU.add)
                        return v.tensor_tensor(
                            y_sb[:, par * 512:(par + 1) * 512],
                            y_sb[:, par * 512:(par + 1) * 512],
                            scrv(XNTM + par * 512, 512), ALU.add)
                    return f
                t_ytt[idx] = prog.emit("vector", ytt(),
                    wait=wl(t_xf, t_rf1, t_rf2, t_shev[si][m], t_yst.get(idx - 2)))
                t_yst[idx] = prog.emit("sync", (lambda si, m, par: lambda g: dma(
                    g, yd[si, m], y_sb[:, par * 512:(par + 1) * 512]))(si, m, par),
                    wait=[t_ytt[idx]], inc=16)

        return _finish(nc, prog, block, sems, debug)

    return nc


def _finish(nc, prog, block, sems, debug):
    qtot = prog.n["sync"]
    prog.emit("sync", lambda g: g.nop(), wait=[("sync", qtot)], inc=0)
    gtot = prog.n["gpsimd"]
    prog.emit("gpsimd", lambda g: g.nop(), wait=[("gpsimd", gtot)], inc=0)
    ttot = prog.n["tensor"]
    prog.emit("tensor", lambda t: t.nop(), wait=[("tensor", ttot)], inc=0)
    vtot = prog.n["vector"]
    prog.emit("vector", lambda v: v.nop(), wait=[("vector", vtot)], inc=0)
    stot = prog.n["scalar"]
    prog.emit("scalar", lambda sc: sc.nop(), wait=[("scalar", stot)], inc=0)

    def runner(eng):
        def body(EE):
            hw = {}
            for fn, waits, inc, coll in prog.ops[eng]:
                for (se, v) in waits:
                    if v > 0 and v > hw.get(se, 0):
                        EE.wait_ge(sems[se], v)
                        hw[se] = v
                inst = fn(EE)
                if coll is not None:
                    inst.then_inc(sems["coll"], 1)
                elif inc:
                    inst.then_inc(sems[eng], inc)
        return body

    block.tensor(runner("tensor"))
    block.vector(runner("vector"))
    block.scalar(runner("scalar"))
    block.sync(runner("sync"))
    block.gpsimd(runner("gpsimd"))
    return nc


_NC_CACHE = {}


def _get_nc(debug=False, upto="fin"):
    key = (bool(debug), upto)
    if key not in _NC_CACHE:
        _NC_CACHE[key] = build_program(debug=key[0], upto=upto)
    return _NC_CACHE[key]


def kernel(L0_attn_x, L0_attn_Wq, L0_attn_Wk, L0_attn_Wv, L0_attn_Wo, L0_attn_bo,
           L0_attn_Wf, L0_attn_bf, L0_ffn_router_w, L0_ffn_wg_r, L0_ffn_wu_r,
           L0_ffn_wd_r, L0_ffn_wg_s, L0_ffn_wu_s, L0_ffn_wd_s, _debug=False,
           _upto="fin"):
    cast = lambda a: np.ascontiguousarray(np.asarray(a, dtype=np.float32))
    x = cast(L0_attn_x)
    wq, wk, wv, wo = cast(L0_attn_Wq), cast(L0_attn_Wk), cast(L0_attn_Wv), cast(L0_attn_Wo)
    bo, wf, bf, rw = cast(L0_attn_bo), cast(L0_attn_Wf), cast(L0_attn_bf), cast(L0_ffn_router_w)
    wg_r, wu_r, wd_r = cast(L0_ffn_wg_r), cast(L0_ffn_wu_r), cast(L0_ffn_wd_r)
    wg_s, wu_s, wd_s = cast(L0_ffn_wg_s), cast(L0_ffn_wu_s), cast(L0_ffn_wd_s)

    nc = _get_nc(debug=_debug, upto=_upto)
    in_maps = []
    for c in range(8):
        b, half = c // 2, c % 2
        xc = x[b, half * TC:(half + 1) * TC, :]
        x_fm = np.ascontiguousarray(xc.T)
        xstg = np.ascontiguousarray(
            x_fm.reshape(D, NSUB, TS).transpose(1, 0, 2))
        in_maps.append({
            "x": xstg, "xt": np.ascontiguousarray(xc), "wq": wq, "wk": wk,
            "wv": wv, "wo": wo,
            "wf": wf, "bfv": bf, "bov": bo, "rw": rw,
            "wg_r": wg_r, "wu_r": wu_r, "wd_r": wd_r,
            "wg_s": wg_s, "wu_s": wu_s, "wd_s": wd_s,
        })
    res = run_bass_kernel_spmd(nc, in_maps, core_ids=list(range(8)))
    out = np.empty((B, S, D), dtype=np.float32)
    for c in range(8):
        b, half = c // 2, c % 2
        yc = np.asarray(res.results[c]["y"])
        y_fm = yc.transpose(1, 2, 0, 3).reshape(D, TC)
        out[b, half * TC:(half + 1) * TC, :] = y_fm.T
    if _debug:
        return out, res
    return out


# version 13 (final)
# speedup vs baseline: 1.2056x; 1.2056x over previous
import contextlib
import numpy as np
import concourse.bass as bass
import concourse.mybir as mybir
from concourse.bass_utils import run_bass_kernel_spmd

F32 = mybir.dt.float32
BF16 = mybir.dt.bfloat16
AF = mybir.ActivationFunctionType
ALU = mybir.AluOpType
AX = mybir.AxisListType.X

D = 1024
H = 16
HD = 64
FD = 64
B = 4
S = 4096
TC = 2048
NSUB = 4
TS = 512
DC = 8
NE = 16
MIDR = 256
NEX = 20

ENGS = ("tensor", "vector", "scalar", "sync", "gpsimd")


class Prog:
    def __init__(self):
        self.ops = {e: [] for e in ENGS}
        self.n = {e: 0 for e in ENGS}
        self.ncoll = 0

    def emit(self, eng, fn, wait=(), inc=1):
        self.ops[eng].append((fn, tuple(wait), inc, None))
        self.n[eng] += inc
        return (eng, self.n[eng])

    def emit_coll(self, fn, wait=()):
        self.ops["gpsimd"].append((fn, tuple(wait), 0, "coll"))
        self.ncoll += 1
        return ("coll", self.ncoll)

    def tok(self, eng):
        return (eng, self.n[eng])


def wl(*toks):
    return [t for t in toks if t is not None]


def build_program(debug=False, upto="fin"):
    nc = bass.Bass("TRN2", target_bir_lowering=False)

    xd = nc.dram_tensor("x", [NSUB, D, TS], F32, kind="ExternalInput").ap()
    wq_d = nc.dram_tensor("wq", [D, D], F32, kind="ExternalInput").ap()
    wk_d = nc.dram_tensor("wk", [D, D], F32, kind="ExternalInput").ap()
    wv_d = nc.dram_tensor("wv", [D, D], F32, kind="ExternalInput").ap()
    wo_d = nc.dram_tensor("wo", [D, D], F32, kind="ExternalInput").ap()
    wf_d = nc.dram_tensor("wf", [HD, FD], F32, kind="ExternalInput").ap()
    bf_d = nc.dram_tensor("bfv", [FD], F32, kind="ExternalInput").ap()
    bo_d = nc.dram_tensor("bov", [D], F32, kind="ExternalInput").ap()
    rw_d = nc.dram_tensor("rw", [D, NE], F32, kind="ExternalInput").ap()
    wgr_d = nc.dram_tensor("wg_r", [NE, 128, DC * MIDR], F32, kind="ExternalInput").ap()
    wur_d = nc.dram_tensor("wu_r", [NE, 128, DC * MIDR], F32, kind="ExternalInput").ap()
    wdr_d = nc.dram_tensor("wd_r", [NE, 128, 2 * D], F32, kind="ExternalInput").ap()
    wgs_d = nc.dram_tensor("wg_s", [2, 2, 128, DC * MIDR], F32, kind="ExternalInput").ap()
    wus_d = nc.dram_tensor("wu_s", [2, 2, 128, DC * MIDR], F32, kind="ExternalInput").ap()
    wds_d = nc.dram_tensor("wd_s", [2, 2, 128, 2 * D], F32, kind="ExternalInput").ap()
    yd = nc.dram_tensor("y", [NSUB, DC, 128, TS], F32, kind="ExternalOutput").ap()
    cc_in = nc.dram_tensor("cc_in", [128, 520], F32)
    cc_out = nc.dram_tensor("cc_out", [128, 520], F32)
    dbg = {}
    if debug:
        dbg["xn"] = nc.dram_tensor("dbg_xn", [DC, 128, TC], F32, kind="ExternalOutput").ap()
        dbg["qf"] = nc.dram_tensor("dbg_qf", [DC, 128, TC], F32, kind="ExternalOutput").ap()
        dbg["kvp"] = nc.dram_tensor("dbg_kvp", [128, 520], F32, kind="ExternalOutput").ap()
        dbg["kvs"] = nc.dram_tensor("dbg_kvs", [128, 520], F32, kind="ExternalOutput").ap()
        dbg["acc"] = nc.dram_tensor("dbg_acc", [DC, 128, TC], F32, kind="ExternalOutput").ap()
        dbg["comb"] = nc.dram_tensor("dbg_comb", [NE, TS], F32, kind="ExternalOutput").ap()
        dbg["lt"] = nc.dram_tensor("dbg_lt", [NE, TS], F32, kind="ExternalOutput").ap()
        dbg["c32"] = nc.dram_tensor("dbg_c32", [128, 64], F32, kind="ExternalOutput").ap()
        dbg["cst"] = nc.dram_tensor("dbg_cst", [128, 112], F32, kind="ExternalOutput").ap()
        dbg["ffn"] = nc.dram_tensor("dbg_ffn", [DC, 128, TC], F32, kind="ExternalOutput").ap()

    prog = Prog()

    with contextlib.ExitStack() as stack:
        E = stack.enter_context

        xn = E(nc.sbuf_tensor("xn", [128, DC * TC], BF16))
        qfa = E(nc.sbuf_tensor("qfa", [128, DC * TC], BF16))
        big = E(nc.sbuf_tensor("big", [128, 32768], BF16))
        scr = E(nc.sbuf_tensor("scr", [128, 20544], BF16))
        wf2 = E(nc.sbuf_tensor("wf2", [128, FD], BF16))
        wfblk = E(nc.sbuf_tensor("wfblk", [128, 128], BF16))
        rw_s = E(nc.sbuf_tensor("rw_s", [128, DC * NE], BF16))
        bf8 = E(nc.sbuf_tensor("bf8", [1, 512], BF16))
        kvbf = E(nc.sbuf_tensor("kvbf", [128, 520], BF16))
        ks2 = E(nc.sbuf_tensor("ks2", [128, 2 * DC], BF16))
        nrbf = E(nc.sbuf_tensor("nrbf", [2, 2 * 512], BF16))
        rowbf = nrbf
        nbsb = E(nc.sbuf_tensor("nbsb", [128, 2 * 512], BF16))
        emask = E(nc.sbuf_tensor("emask", [16, 16 * 128], BF16))
        hmask = E(nc.sbuf_tensor("hmask", [2, 128], BF16))
        ones1 = E(nc.sbuf_tensor("ones1", [1, 128], BF16))
        ones128 = E(nc.sbuf_tensor("ones128", [128, 1], BF16))
        cfm = E(nc.sbuf_tensor("cfm", [16, NSUB * TS], BF16))
        rb_sb = E(nc.sbuf_tensor("rb_sb", [128, 512], BF16))
        murb_sb = E(nc.sbuf_tensor("murb_sb", [128, 512], BF16))
        bf128 = E(nc.sbuf_tensor("bf128", [128, 1], F32))
        bo_sb = E(nc.sbuf_tensor("bo_sb", [128, DC], F32))
        kvsb = E(nc.sbuf_tensor("kvsb", [128, 520], F32))
        lt32 = E(nc.sbuf_tensor("lt32", [16, 512], F32))
        comb32 = E(nc.sbuf_tensor("comb32", [128, 64], F32))
        cst = E(nc.sbuf_tensor("cst", [128, 112], F32))
        ident32 = E(nc.sbuf_tensor("ident32", [128, 128], F32))
        pid32 = E(nc.sbuf_tensor("pid32", [128, 1], F32))
        eps128 = E(nc.sbuf_tensor("eps128", [128, 1], F32))
        srow0 = E(nc.sbuf_tensor("srow0", [1, 512], F32))
        srow1 = E(nc.sbuf_tensor("srow1", [1, 512], F32))
        srow2 = E(nc.sbuf_tensor("srow2", [1, 512], F32))
        nr32 = E(nc.sbuf_tensor("nr32", [2, 512], F32))
        xf_sb = E(nc.sbuf_tensor("xf_sb", [128, 2 * 512], F32))
        y_sb = E(nc.sbuf_tensor("y_sb", [128, 2 * 512], F32))

        PS = [E(nc.psum_tensor(f"P{i}", [128, 512], F32)) for i in range(8)]

        sems = {}
        for e in ENGS:
            sems[e] = E(nc.semaphore(f"sem_{e}"))
        sems["coll"] = E(nc.semaphore("sem_coll"))

        block = E(nc.Block())

        WQ, WK, WV, WO = 0, 8192, 16384, 24576
        WGS, WUS, WDS, HARE = 0, 4096, 8192, 12288
        QS, KS, KFT, VAUG, XSB, X2 = 0, 0, 4096, 8192, 12352, 16448
        MIX = 0
        GT, UT, CB = 4096, 6144, 8192

        def bigv(off, n):
            return big[:, off:off + n]

        def scrv(off, n):
            return scr[:, off:off + n]

        prog.emit("gpsimd", lambda g: g.iota(
            pid32[:, 0:1], [[1, 1]], channel_multiplier=1,
            allow_small_or_imprecise_dtypes=True))
        prog.emit("gpsimd", lambda g: g.iota(
            ident32[:, :], [[1, 128]], channel_multiplier=0,
            allow_small_or_imprecise_dtypes=True))
        prog.emit("gpsimd", lambda g: g.iota(
            emask[:, :], [[1, 16], [0, 128]], channel_multiplier=0,
            allow_small_or_imprecise_dtypes=True))
        prog.emit("gpsimd", lambda g: g.iota(
            hmask[:, :], [[1, 128]], channel_multiplier=0,
            allow_small_or_imprecise_dtypes=True))
        prog.emit("gpsimd", lambda g: g.memset(ones1[:, :], 1.0))
        prog.emit("gpsimd", lambda g: g.memset(eps128[:, :], 1e-5))
        prog.emit("gpsimd", lambda g: g.memset(ones128[:, :], 1.0))
        prog.emit("gpsimd", lambda g: g.memset(
            scrv(VAUG, 4160).rearrange("p (tc h e) -> p tc h e", tc=4, h=16)[:, :, :, 64:65],
            1.0))
        tg_init = prog.tok("gpsimd")

        t_id = prog.emit("vector", lambda v: v.tensor_scalar(
            ident32[:, :], ident32[:, :], pid32[:, 0:1], None, ALU.is_equal),
            wait=[tg_init])
        t_em = prog.emit("vector", lambda v: v.tensor_scalar(
            emask[:, :], emask[:, :], pid32[0:16, 0:1], None, ALU.is_equal))
        t_hm1 = prog.emit("vector", lambda v: v.tensor_scalar(
            hmask[:, :], hmask[:, :], 64.0, None, ALU.is_ge))
        t_hm = prog.emit("vector", lambda v: v.tensor_scalar(
            hmask[:, :], hmask[:, :], pid32[0:2, 0:1], None, ALU.is_equal),
            wait=[t_hm1])

        def dma(g, out, in_):
            return g.dma_start(out=out, in_=in_)

        t_bf8 = None
        for i in range(8):
            t_bf8 = prog.emit("gpsimd", (lambda i: lambda g: dma(
                g, bf8[0:1, i * 64:(i + 1) * 64], bf_d[None, :]))(i), inc=16)
        prog.emit("sync", lambda g: dma(g, bf128[0:64, 0:1], bf_d[:, None]), inc=16)
        t_bf128 = prog.emit("sync", lambda g: dma(g, bf128[64:128, 0:1], bf_d[:, None]),
                            inc=16)
        t_bo = prog.emit("sync", lambda g: g.dma_start(
            out=bo_sb[:, :], in_=bo_d.rearrange("(m p) -> p m", p=128),
            allow_slow_non_contiguous=True), inc=16)
        prog.emit("gpsimd", lambda g: g.memset(wfblk[:, :], 0.0))
        prog.emit("gpsimd", lambda g: dma(g, wfblk[0:64, 0:64], wf_d[:, :]), inc=16)
        prog.emit("gpsimd", lambda g: dma(g, wfblk[64:128, 64:128], wf_d[:, :]), inc=16)
        prog.emit("gpsimd", lambda g: dma(g, wf2[0:64, :], wf_d[:, :]), inc=16)
        t_wf2 = prog.emit("gpsimd", lambda g: dma(g, wf2[64:128, :], wf_d[:, :]), inc=16)
        t_rw = prog.emit("gpsimd", lambda g: dma(
            g, rw_s[:, :].rearrange("p (k e) -> p k e", k=DC),
            rw_d.rearrange("(k p) e -> p k e", p=128)), inc=16)

        def w_load(dst_off, src):
            tk = None
            for k in range(DC):
                tk = prog.emit("gpsimd", (lambda k: lambda g: dma(
                    g, bigv(dst_off + k * 1024, 1024), src[k * 128:(k + 1) * 128, :]))(k),
                    inc=16)
            return tk

        t_x0pre = prog.emit("gpsimd", lambda g: dma(
            g, scrv(XSB, 4096).rearrange("p (c t) -> p c t", c=DC),
            xd[0].rearrange("(c p) t -> p c t", p=128)), inc=16)
        t_wq = w_load(WQ, wq_d)
        t_wk = w_load(WK, wk_d)
        t_wv = w_load(WV, wv_d)
        t_wo = w_load(WO, wo_d)

        LVL = {"init": 0, "A1": 1, "A2": 1, "A3": 1, "A4": 1, "A": 1,
               "coll": 2, "B": 3, "ffn": 4, "fin": 5}[upto]
        ALVL = {"A1": 1, "A2": 2, "A3": 3, "A4": 4}.get(upto, 5)
        bank_free = [None] * 8
        guardA = {}
        t_x2 = [None] * NSUB
        t_stats = [None] * NSUB
        t_xload = [None] * NSUB
        t_xn = [None] * NSUB
        t_qf = [None] * DC
        t_kv_last = None
        prev_rbmm = None
        prev_mumm = None

        RC_tok = {}
        RC_bank = {}
        t_cfmQ = [None] * NSUB

        def emit_rc_a(Q, bank=7):
            RC_bank[Q] = bank
            def rmm(Q=Q):
                def f(te):
                    last = None
                    for k in range(DC):
                        last = te.matmul(PS[bank][0:16, :], rw_s[:, k * 16:(k + 1) * 16],
                                         xn[:, k * TC + Q * TS: k * TC + Q * TS + TS],
                                         start=(k == 0), stop=(k == DC - 1))
                    return last
                return f
            t_r = prog.emit("tensor", rmm(),
                            wait=wl(t_rw if Q == 0 else None, bank_free[bank]))
            t_lt = prog.emit("scalar", lambda sc: sc.copy(out=lt32[:, :],
                                                          in_=PS[bank][0:16, :]),
                             wait=[t_r])

            def trl():
                def f(te):
                    last = None
                    for tc in range(4):
                        last = te.transpose(PS[bank][:, tc * 16:(tc + 1) * 16],
                                            lt32[0:16, tc * 128:(tc + 1) * 128],
                                            ident32[0:16, 0:16])
                    return last
                return f
            t_tr = prog.emit("tensor", trl(), wait=wl(t_lt, t_id if Q == 0 else None))
            t_cmb = [None] * 4
            for tc in range(4):
                def cmb_a(tc=tc):
                    def f(v):
                        v.reduce_max(cst[:, 0:1], PS[bank][:, tc * 16:(tc + 1) * 16],
                                     axis=AX)
                        v.drain()
                        v.tensor_scalar(cst[:, 16:32],
                                        PS[bank][:, tc * 16:(tc + 1) * 16],
                                        cst[:, 0:1], None, ALU.subtract)
                        v.drain()
                        v.tensor_scalar(cst[:, 32:48], cst[:, 16:32], 0.0, None, ALU.is_ge)
                        v.drain()
                        v.scalar_tensor_tensor(cst[:, 48:64], cst[:, 32:48], -1e30,
                                               cst[:, 16:32], ALU.mult, ALU.add)
                        v.drain()
                        return v.reduce_max(cst[:, 1:2], cst[:, 48:64], axis=AX)
                    return f
                t_a = prog.emit("vector", cmb_a(), wait=[t_tr])
                t_e = prog.emit("scalar", lambda sc: sc.activation(
                    out=cst[:, 96:112], in_=cst[:, 16:32], func=AF.Exp), wait=[t_a])
                def cmb_b(tc=tc):
                    def f(v):
                        v.drain()
                        v.tensor_scalar(cst[:, 64:80], cst[:, 16:32], cst[:, 1:2], None,
                                        ALU.is_ge)
                        v.drain()
                        v.tensor_tensor(cst[:, 80:96], cst[:, 96:112], cst[:, 64:80],
                                        ALU.mult)
                        v.drain()
                        v.reduce_sum(cst[:, 2:3], cst[:, 80:96], axis=AX)
                        v.drain()
                        v.reciprocal(cst[:, 3:4], cst[:, 2:3])
                        v.drain()
                        return v.tensor_scalar(comb32[:, tc * 16:(tc + 1) * 16],
                                               cst[:, 80:96], cst[:, 3:4], None, ALU.mult)
                    return f
                t_cmb[tc] = prog.emit("vector", cmb_b(), wait=[t_e])
            RC_tok[Q] = t_cmb

        def emit_rc_b(Q):
            t_cmb = RC_tok[Q]
            bank = RC_bank[Q]

            def trc():
                def f(te):
                    last = None
                    for tc in range(4):
                        last = te.transpose(PS[bank][0:16, tc * 128:(tc + 1) * 128],
                                            comb32[:, tc * 16:(tc + 1) * 16],
                                            ident32[:, :])
                    return last
                return f
            t_trc = prog.emit("tensor", trc(), wait=wl(*t_cmb))
            t_cfmQ[Q] = prog.emit("scalar", (lambda Q: lambda sc: sc.copy(
                out=cfm[0:16, Q * TS:(Q + 1) * TS], in_=PS[bank][0:16, :]))(Q),
                wait=wl(t_trc))
            bank_free[bank] = t_cfmQ[Q]

        def XS(s):
            return XSB + (s % 2) * 4096

        def X2S(s):
            return XSB + ((s + 1) % 2) * 4096

        LN_tok = {}

        def emit_LN_a(s):
            nonlocal prev_rbmm, prev_mumm
            if s == 0:
                t_xload[0] = t_x0pre
            else:
                t_xload[s] = prog.emit("gpsimd", (lambda s: lambda g: dma(
                    g, scrv(XS(s), 4096).rearrange("p (c t) -> p c t", c=DC),
                    xd[s].rearrange("(c p) t -> p c t", p=128)))(s),
                    wait=wl(t_stats[s - 1]), inc=16)
            t_x2[s] = prog.emit("scalar", (lambda s: lambda sc: sc.activation(
                out=scrv(X2S(s), 4096), in_=scrv(XS(s), 4096), func=AF.Square))(s),
                wait=wl(t_xload[s], t_xn[s - 1] if s > 0 else None))

            def stats_fn(s=s):
                def f(te):
                    last = None
                    for c in range(DC):
                        last = te.matmul(PS[5][0:1, :], ones128[:, 0:1],
                                         scrv(XS(s) + c * 512, 512),
                                         start=(c == 0), stop=(c == DC - 1))
                    for c in range(DC):
                        last = te.matmul(PS[4][0:1, :], ones128[:, 0:1],
                                         scrv(X2S(s) + c * 512, 512),
                                         start=(c == 0), stop=(c == DC - 1))
                    return last
                return f
            t_stats[s] = prog.emit("tensor", stats_fn(),
                                   wait=wl(t_x2[s], bank_free[5], bank_free[4]))
            t_p5free = prog.emit("vector", lambda v: v.tensor_scalar(
                srow0[0:1, :], PS[5][0:1, :], 1.0 / D, None, ALU.mult),
                wait=[t_stats[s]])
            bank_free[5] = t_p5free
            t_p4free = prog.emit("vector", lambda v: v.tensor_scalar(
                srow1[0:1, :], PS[4][0:1, :], 1.0 / D, None, ALU.mult))
            prog.emit("vector", lambda v: v.tensor_tensor(
                srow2[0:1, :], srow0[0:1, :], srow0[0:1, :], ALU.mult))
            t_var = prog.emit("vector", lambda v: v.tensor_tensor(
                srow1[0:1, :], srow1[0:1, :], srow2[0:1, :], ALU.subtract))
            prog.emit("scalar", lambda sc: sc.activation(
                out=srow1[0:1, :], in_=srow1[0:1, :], func=AF.Ln,
                bias=eps128[0:1, 0:1]),
                wait=wl(t_var, tg_init if s == 0 else None))
            t_rstd = prog.emit("scalar", lambda sc: sc.activation(
                out=srow2[0:1, :], in_=srow1[0:1, :], func=AF.Exp, scale=-0.5))
            prog.emit("vector", lambda v: v.tensor_tensor(
                srow1[0:1, :], srow0[0:1, :], srow2[0:1, :], ALU.mult),
                wait=[t_rstd])
            prog.emit("vector", lambda v: v.tensor_copy(rowbf[0:1, 0:512], srow2[0:1, :]),
                      wait=wl(prev_rbmm))
            t_rowbf = prog.emit("vector", lambda v: v.tensor_copy(
                rowbf[0:1, 512:1024], srow1[0:1, :]), wait=wl(prev_mumm))
            LN_tok[s] = (t_p4free, t_rowbf)

        def emit_LN_b(s):
            nonlocal prev_rbmm, prev_mumm
            t_p4free, t_rowbf = LN_tok[s]
            t_rbmm = prog.emit("tensor", lambda te: te.matmul(
                PS[4][:, :], ones1[0:1, :], rowbf[0:1, 0:512], start=True, stop=True),
                wait=wl(t_rowbf, t_p4free, bank_free[4]))
            t_mumm = prog.emit("tensor", lambda te: te.matmul(
                PS[5][:, :], ones1[0:1, :], rowbf[0:1, 512:1024], start=True, stop=True),
                wait=wl(bank_free[5]))
            prev_rbmm, prev_mumm = t_rbmm, t_mumm
            t_rbev = prog.emit("scalar", lambda sc: sc.copy(out=rb_sb[:, :],
                                                            in_=PS[4][:, :]),
                               wait=[t_rbmm])
            bank_free[4] = t_rbev
            t_muev = prog.emit("scalar", lambda sc: sc.copy(out=murb_sb[:, :],
                                                            in_=PS[5][:, :]),
                               wait=[t_mumm])
            bank_free[5] = t_muev
            for c in range(DC):
                prog.emit("vector", (lambda c, s: lambda v: v.tensor_tensor(
                    xn[:, c * TC + s * TS: c * TC + s * TS + TS],
                    scrv(XS(s) + c * 512, 512), rb_sb[:, :], ALU.mult))(c, s),
                    wait=[t_rbev] if c == 0 else ())
                t_xn[s] = prog.emit("vector", (lambda c, s: lambda v: v.tensor_tensor(
                    xn[:, c * TC + s * TS: c * TC + s * TS + TS],
                    xn[:, c * TC + s * TS: c * TC + s * TS + TS],
                    murb_sb[:, :], ALU.subtract))(c, s),
                    wait=[t_muev] if c == 0 else ())

        if LVL >= 1:
            emit_LN_a(0)
            emit_LN_b(0)
        for s in range(NSUB if LVL >= 1 else 0):
            t_qevac = [None] * DC
            for m in range(DC if ALVL >= 2 else 0):
                bk = m % 2
                def qmm(m=m, s=s, bk=bk):
                    def f(te):
                        last = None
                        for k in range(DC):
                            last = te.matmul(
                                PS[bk][:, :],
                                bigv(WQ + k * 1024 + m * 128, 128),
                                xn[:, k * TC + s * TS: k * TC + s * TS + TS],
                                start=(k == 0), stop=(k == DC - 1))
                        return last
                    return f
                t_mm = prog.emit("tensor", qmm(),
                                 wait=wl(t_xn[s], t_wq if s == 0 and m == 0 else None,
                                         bank_free[bk]))
                t_qevac[m] = prog.emit("scalar", (lambda m, bk: lambda sc: sc.copy(
                    out=scrv(QS + m * 512, 512), in_=PS[bk][:, :]))(m, bk),
                    wait=[t_mm])
                bank_free[bk] = t_qevac[m]
            for cp in range(DC if ALVL >= 2 else 0):
                bk = 2 + cp % 2
                def qfmm(cp=cp, bk=bk):
                    def f(te):
                        te.matmul(PS[bk][0:64, :], wf2[0:64, :],
                                  scr[0:64, QS + cp * 512: QS + (cp + 1) * 512],
                                  start=True, stop=True, tile_position=(0, 0))
                        return te.matmul(PS[bk][64:128, :], wf2[64:128, :],
                                         scr[64:128, QS + cp * 512: QS + (cp + 1) * 512],
                                         start=True, stop=True, tile_position=(64, 64))
                    return f
                t_mm = prog.emit("tensor", qfmm(),
                                 wait=wl(t_qevac[cp],
                                         t_wf2 if s == 0 and cp == 0 else None,
                                         bank_free[bk]))
                t_qf[cp] = prog.emit("scalar", (lambda cp, bk, s: lambda sc: sc.activation(
                    out=qfa[:, cp * TC + s * TS: cp * TC + s * TS + TS],
                    in_=PS[bk][:, :], func=AF.Relu, bias=bf128[:, 0:1]))(cp, bk, s),
                    wait=wl(t_mm, t_bf128 if s == 0 and cp == 0 else None))
                bank_free[bk] = t_qf[cp]

            if s + 1 < NSUB:
                emit_LN_a(s + 1)
            t_kevac = [None] * DC
            for m in range(DC if ALVL >= 3 else 0):
                bk = m % 2
                def kmm(m=m, s=s, bk=bk):
                    def f(te):
                        last = None
                        for k in range(DC):
                            last = te.matmul(
                                PS[bk][:, :],
                                bigv(WK + k * 1024 + m * 128, 128),
                                xn[:, k * TC + s * TS: k * TC + s * TS + TS],
                                start=(k == 0), stop=(k == DC - 1))
                        return last
                    return f
                t_mm = prog.emit("tensor", kmm(),
                                 wait=wl(t_wk if s == 0 and m == 0 else None,
                                         bank_free[bk]))
                t_kevac[m] = prog.emit("scalar", (lambda m, bk: lambda sc: sc.copy(
                    out=scrv(KS + m * 512, 512), in_=PS[bk][:, :]))(m, bk),
                    wait=[t_mm])
                bank_free[bk] = t_kevac[m]
            if s == NSUB - 1:
                guardA["k"] = t_kevac[DC - 1]
            t_kf = [None] * 4
            for tc in range(4 if ALVL >= 3 else 0):
                def kfmm(tc=tc):
                    def f(te):
                        last = None
                        for bk, p0 in ((2, 0), (3, 4)):
                            for pp in range(4):
                                pair = p0 + pp
                                te.matmul(
                                    PS[bk][:, pp * 128:(pp + 1) * 128],
                                    scrv(KS + pair * 512 + tc * 128, 128),
                                    wfblk[:, :],
                                    start=(pp == 0), stop=False)
                            last = te.matmul(PS[bk][:, :], ones1[0:1, :], bf8[0:1, :],
                                             start=False, stop=True)
                        return last
                    return f
                t_mm = prog.emit("tensor", kfmm(),
                                 wait=wl(t_kevac[7],
                                         t_bf8 if s == 0 and tc == 0 else None,
                                         bank_free[2], bank_free[3]))
                ev2 = prog.emit("scalar", (lambda tc: lambda sc: sc.activation(
                    out=scrv(KFT + tc * 1024, 512), in_=PS[2][:, :], func=AF.Relu))(tc),
                    wait=[t_mm])
                ev3 = prog.emit("scalar", (lambda tc: lambda sc: sc.activation(
                    out=scrv(KFT + tc * 1024 + 512, 512), in_=PS[3][:, :],
                    func=AF.Relu))(tc))
                bank_free[2] = ev2
                bank_free[3] = ev3
                t_kf[tc] = ev3

            if s + 1 < NSUB:
                emit_LN_b(s + 1)
            t_va = [[None, None] for _ in range(4)]
            for tc in range(4 if ALVL >= 4 else 0):
                for half in range(2):
                    bk = (tc * 2 + half) % 2
                    def vmm(tc=tc, half=half, s=s, bk=bk):
                        def f(te):
                            last = None
                            for k in range(DC):
                                last = te.matmul(
                                    PS[bk][:, :],
                                    xn[:, k * TC + s * TS + tc * 128:
                                       k * TC + s * TS + (tc + 1) * 128],
                                    bigv(WV + k * 1024 + half * 512, 512),
                                    start=(k == 0), stop=(k == DC - 1))
                            return last
                        return f
                    t_mm = prog.emit("tensor", vmm(),
                                     wait=wl(t_wv if s == 0 and tc == 0 and half == 0
                                             else None,
                                             bank_free[bk]))
                    t_va[tc][half] = prog.emit("vector",
                        (lambda tc, half, bk: lambda v: v.tensor_copy(
                            scrv(VAUG + tc * 1040, 1040)
                            .rearrange("p (h e) -> p h e", h=16)
                            [:, half * 8:(half + 1) * 8, 0:64],
                            PS[bk][:, :].rearrange("p (h d) -> p h d", h=8)))(tc, half, bk),
                        wait=[t_mm])
                    bank_free[bk] = t_va[tc][half]

            if s == NSUB - 1 and LVL >= 4:
                emit_rc_a(0, bank=5)
                emit_rc_b(0)
            for tc in range(4 if ALVL >= 5 else 0):
                def kvmm(tc=tc, s=s):
                    def f(te):
                        last = None
                        for h in range(H):
                            pair = h // 2
                            bk = 6 + pair // 4
                            col = (pair % 4) * 65
                            r0 = (h % 2) * 64
                            first = (s == 0 and tc == 0 and h in (0, 1, 8, 9))
                            last = te.matmul(
                                PS[bk][r0:r0 + 64, col:col + 65],
                                scrv(KFT + tc * 1024 + h * 64, 64),
                                scrv(VAUG + tc * 1040 + h * 65, 65),
                                start=first, stop=False,
                                tile_position=(0, r0))
                        return last
                    return f
                t_kv_last = prog.emit("tensor", kvmm(),
                                      wait=wl(t_kf[tc], t_va[tc][0], t_va[tc][1]))

        if LVL < 2:
            return _finish(nc, prog, block, sems, debug)
        prog.emit("vector", lambda v: v.tensor_copy(kvsb[:, 0:260], PS[6][:, 0:260]),
                  wait=[t_kv_last])
        t_kvev = prog.emit("vector", lambda v: v.tensor_copy(kvsb[:, 260:520],
                                                             PS[7][:, 0:260]))
        bank_free[6] = t_kvev
        bank_free[7] = t_kvev
        t_ccin = prog.emit("sync", lambda g: dma(g, cc_in[:, :], kvsb[:, :]),
                           wait=[t_kvev], inc=16)
        if debug:
            prog.emit("gpsimd", lambda g: dma(g, dbg["kvp"][:, :], kvsb[:, :]),
                      wait=[t_kvev], inc=16)
        t_coll = prog.emit_coll(lambda g: g.collective_compute(
            "AllReduce", ALU.add,
            replica_groups=[[0, 1], [2, 3], [4, 5], [6, 7]],
            ins=[cc_in.ap()],
            outs=[cc_out.ap()]),
            wait=[t_ccin])
        t_kvsum = prog.emit("sync", lambda g: dma(g, kvsb[:, :], cc_out[:, :]),
                            wait=[t_coll], inc=16)
        t_kvbf = None
        t_ks2 = None
        t_dbgqf = None

        def emit_kv_post():
            nonlocal t_kvbf, t_ks2, t_dbgqf
            t_kvbf = prog.emit("vector", lambda v: v.tensor_copy(kvbf[:, :], kvsb[:, :]),
                               wait=[t_kvsum])
            if debug:
                prog.emit("gpsimd", lambda g: dma(g, dbg["kvs"][:, :], kvsb[:, :]),
                          wait=[t_kvbf], inc=16)
                for c in range(DC):
                    prog.emit("gpsimd", (lambda c: lambda g: dma(
                        g, dbg["xn"][c], xn[:, c * TC:(c + 1) * TC]))(c),
                        wait=[t_xn[3]], inc=16)
                    t_dbgqf = prog.emit("gpsimd", (lambda c: lambda g: dma(
                        g, dbg["qf"][c], qfa[:, c * TC:(c + 1) * TC]))(c),
                        wait=[t_qf[c]], inc=16)
            prog.emit("vector", lambda v: v.memset(ks2[:, :], 0.0), wait=[t_kvbf])
            for c in range(DC):
                prog.emit("vector", (lambda c: lambda v: v.tensor_copy(
                    ks2[0:64, 2 * c:2 * c + 1], kvbf[0:64, c * 65 + 64: c * 65 + 65]))(c))
                t_ks2 = prog.emit("vector", (lambda c: lambda v: v.tensor_copy(
                    ks2[64:128, 2 * c + 1:2 * c + 2],
                    kvbf[64:128, c * 65 + 64: c * 65 + 65]))(c))

        if LVL < 3:
            if LVL >= 2:
                emit_kv_post()
            return _finish(nc, prog, block, sems, debug)
        t_accB = [[None] * DC for _ in range(NSUB)]
        DBGACC = [None]

        def emit_phase_b():
            t_wolast = None
            prev_nb = {}
            prev_bc = {}
            for s in range(NSUB):
                t_mix = [None] * DC
                for c in range(DC):
                    po = 0 if c % 2 == 0 else 5
                    pn = 1 if c % 2 == 0 else 3
                    pb = 2 if c % 2 == 0 else 4
                    par = c % 2
                    def outmm(c=c, s=s, po=po):
                        def f(te):
                            te.matmul(PS[po][0:64, :], kvbf[0:64, c * 65: c * 65 + 64],
                                      qfa[0:64, c * TC + s * TS: c * TC + s * TS + TS],
                                      start=True, stop=True, tile_position=(0, 0))
                            return te.matmul(PS[po][64:128, :],
                                             kvbf[64:128, c * 65: c * 65 + 64],
                                             qfa[64:128, c * TC + s * TS: c * TC + s * TS + TS],
                                             start=True, stop=True, tile_position=(64, 64))
                        return f
                    t_out = prog.emit("tensor", outmm(),
                                      wait=wl(t_kvbf, t_dbgqf, bank_free[po]))
                    def nmm(c=c, s=s, pn=pn):
                        return lambda te: te.matmul(
                            PS[pn][0:2, :], ks2[:, 2 * c:2 * c + 2],
                            qfa[:, c * TC + s * TS: c * TC + s * TS + TS],
                            start=True, stop=True)
                    t_nmm = prog.emit("tensor", nmm(), wait=wl(t_ks2, bank_free[pn]))
                    t_nm = prog.emit("vector", (lambda pn: lambda v: v.tensor_scalar(
                        nr32[0:2, :], PS[pn][0:2, :], 1e-6, None, ALU.max))(pn),
                        wait=[t_nmm])
                    bank_free[pn] = t_nm
                    prog.emit("scalar", lambda sc: sc.activation(
                        out=nr32[0:2, :], in_=nr32[0:2, :], func=AF.Ln),
                        wait=[t_nm])
                    t_nr = prog.emit("scalar", (lambda par: lambda sc: sc.activation(
                        out=nrbf[0:2, par * 512:(par + 1) * 512], in_=nr32[0:2, :],
                        func=AF.Exp, scale=-1.0))(par),
                        wait=wl(prev_bc.get(par)))
                    t_bc = prog.emit("tensor", (lambda par, pb: lambda te: te.matmul(
                        PS[pb][:, :], hmask[0:2, :], nrbf[0:2, par * 512:(par + 1) * 512],
                        start=True, stop=True))(par, pb),
                        wait=wl(t_nr, t_hm if s == 0 and c < 2 else None, bank_free[pb]))
                    prev_bc[par] = t_bc
                    t_nb = prog.emit("scalar", (lambda par, pb: lambda sc: sc.copy(
                        out=nbsb[:, par * 512:(par + 1) * 512], in_=PS[pb][:, :]))(par, pb),
                        wait=wl(t_bc, prev_nb.get(par)))
                    bank_free[pb] = t_nb
                    t_mix[c] = prog.emit("vector", (lambda c, par, po: lambda v: v.tensor_tensor(
                        scrv(MIX + c * 512, 512), PS[po][:, :],
                        nbsb[:, par * 512:(par + 1) * 512], ALU.mult))(c, par, po),
                        wait=wl(t_nb, t_out, t_wolast))
                    prev_nb[par] = t_mix[c]
                    bank_free[po] = t_mix[c]
                for m in range(DC):
                    bk = 1 if m % 2 == 0 else 3
                    def womm(m=m, bk=bk):
                        def f(te):
                            last = None
                            for k in range(DC):
                                last = te.matmul(PS[bk][:, :],
                                                 bigv(WO + k * 1024 + m * 128, 128),
                                                 scrv(MIX + k * 512, 512),
                                                 start=(k == 0), stop=(k == DC - 1))
                            return last
                        return f
                    t_mm = prog.emit("tensor", womm(),
                                     wait=wl(t_wo if m == 0 and s == 0 else None,
                                             t_mix[7], bank_free[bk]))
                    if m == DC - 1:
                        t_wolast = t_mm
                    t_accB[s][m] = prog.emit("scalar", (lambda m, s, bk: lambda sc: sc.activation(
                        out=qfa[:, m * TC + s * TS: m * TC + s * TS + TS],
                        in_=PS[bk][:, :], func=AF.Identity, bias=bo_sb[:, m:m + 1]))(m, s, bk),
                        wait=wl(t_mm, t_bo if m == 0 and s == 0 else None))
                    bank_free[bk] = t_accB[s][m]
            t_dbgacc = None
            if debug:
                for c in range(DC):
                    t_dbgacc = prog.emit("gpsimd", (lambda c: lambda g: dma(
                        g, dbg["acc"][c], qfa[:, c * TC:(c + 1) * TC]))(c),
                        wait=[t_accB[3][c]], inc=16)

        if LVL < 4:
            if LVL >= 3:
                emit_kv_post()
                emit_phase_b()
            return _finish(nc, prog, block, sems, debug)
        t_ffn_ev = [[None] * DC for _ in range(NSUB)]
        t_gmm = {}
        t_umm = {}
        t_dmm = {}
        t_htok = {}
        t_gev = {}
        t_uev = {}
        t_cbev = {}
        t_cbmm_last = {}
        t_dbgcomb = None
        st_first_done = [False]

        def emit_expert(Q, e):
            par = e % 2
            if e < NE:
                wg_src = wgr_d[e]
                wu_src = wur_d[e]
            else:
                j = e - NE
                n, hf = j // 2, j % 2
                wg_src = wgs_d[n][hf]
                wu_src = wus_d[n][hf]
            guard = guardA.get("q") if (Q == 0 and e < 2) else None
            t_wg = prog.emit("gpsimd", (lambda src=wg_src, par=par: lambda g: dma(
                g, bigv(WGS + par * 2048, 2048), src))(),
                wait=wl(t_gmm.get((Q, e - 2)), guard,
                        t_gmm.get((Q - 1, NEX - 2 + par))), inc=16)
            t_wu = prog.emit("gpsimd", (lambda src=wu_src, par=par: lambda g: dma(
                g, bigv(WUS + par * 2048, 2048), src))(),
                wait=wl(t_umm.get((Q, e - 2)), guard,
                        t_umm.get((Q - 1, NEX - 2 + par))), inc=16)
            if e < NE:
                t_cbmm = prog.emit("tensor", (lambda e, Q: lambda te: te.matmul(
                    PS[4][:, :], emask[0:16, e * 128:(e + 1) * 128],
                    cfm[0:16, Q * TS:(Q + 1) * TS],
                    start=True, stop=True))(e, Q),
                    wait=wl(t_cfmQ[Q], t_em if Q == 0 and e == 0 else None,
                            bank_free[4]))
                t_cbmm_last[Q] = t_cbmm
                t_cbev[(Q, e)] = prog.emit("scalar", (lambda par: lambda sc: sc.copy(
                    out=scrv(CB + par * 512, 512), in_=PS[4][:, :]))(par),
                    wait=wl(t_cbmm, t_uev.get((Q, e - 2))))
                bank_free[4] = t_cbev[(Q, e)]

            def gmm(Q=Q, par=par):
                def f(te):
                    last = None
                    for mc in range(2):
                        for k in range(DC):
                            last = te.matmul(
                                PS[mc][:, :],
                                bigv(WGS + par * 2048 + k * 256 + mc * 128, 128),
                                xn[:, k * TC + Q * TS: k * TC + Q * TS + TS],
                                start=(k == 0), stop=(k == DC - 1))
                    return last
                return f
            t_gmm[(Q, e)] = prog.emit("tensor", gmm(),
                                      wait=wl(t_wg, bank_free[0], bank_free[1]))
            prog.emit("scalar", (lambda par: lambda sc: sc.activation(
                out=scrv(GT + par * 1024, 512), in_=PS[0][:, :], func=AF.Silu))(par),
                wait=wl(t_gmm[(Q, e)], t_htok.get((Q, e - 2))))
            t_gev2 = prog.emit("scalar", (lambda par: lambda sc: sc.activation(
                out=scrv(GT + par * 1024 + 512, 512), in_=PS[1][:, :],
                func=AF.Silu))(par))
            t_gev[(Q, e)] = t_gev2
            bank_free[0] = t_gev2
            bank_free[1] = t_gev2

            def umm(Q=Q, par=par):
                def f(te):
                    last = None
                    for mc in range(2):
                        for k in range(DC):
                            last = te.matmul(
                                PS[2 + mc][:, :],
                                bigv(WUS + par * 2048 + k * 256 + mc * 128, 128),
                                xn[:, k * TC + Q * TS: k * TC + Q * TS + TS],
                                start=(k == 0), stop=(k == DC - 1))
                    return last
                return f
            t_umm[(Q, e)] = prog.emit("tensor", umm(),
                                      wait=wl(t_wu, bank_free[2], bank_free[3]))
            if e < NE:
                prog.emit("vector", (lambda par: lambda v: v.tensor_tensor(
                    scrv(UT + par * 1024, 512), PS[2][:, :],
                    scrv(CB + par * 512, 512), ALU.mult))(par),
                    wait=wl(t_umm[(Q, e)], t_cbev[(Q, e)]))
                t_u2 = prog.emit("vector", (lambda par: lambda v: v.tensor_tensor(
                    scrv(UT + par * 1024 + 512, 512), PS[3][:, :],
                    scrv(CB + par * 512, 512), ALU.mult))(par))
            else:
                prog.emit("vector", (lambda par: lambda v: v.tensor_copy(
                    scrv(UT + par * 1024, 512), PS[2][:, :]))(par),
                    wait=[t_umm[(Q, e)]])
                t_u2 = prog.emit("vector", (lambda par: lambda v: v.tensor_copy(
                    scrv(UT + par * 1024 + 512, 512), PS[3][:, :]))(par))
            t_uev[(Q, e)] = t_u2
            bank_free[2] = t_u2
            bank_free[3] = t_u2
            t_htok[(Q, e)] = prog.emit("vector",
                (lambda e, par: lambda v: v.tensor_tensor(
                    bigv(HARE + e * 1024, 1024), scrv(GT + par * 1024, 1024),
                    scrv(UT + par * 1024, 1024), ALU.mult))(e, par),
                wait=wl(t_gev2, t_dmm.get((Q - 1, e))))

        def emit_down(Q):
            for e in range(NEX):
                par = e % 2
                if e < NE:
                    wd_src = wdr_d[e]
                else:
                    j = e - NE
                    n, hf = j // 2, j % 2
                    wd_src = wds_d[n][hf]
                guard = guardA.get("k") if (Q == 0 and e < 2) else None
                t_wd = prog.emit("gpsimd", (lambda src=wd_src, par=par: lambda g: dma(
                    g, bigv(WDS + par * 2048, 2048), src))(),
                    wait=wl(t_dmm.get((Q, e - 2)), guard,
                            t_dmm.get((Q - 1, NEX - 2 + par))), inc=16)
                w0 = wl(*[bank_free[i] for i in range(8)]) if e == 0 else []
                def dmm(Q=Q, e=e, par=par):
                    def f(te):
                        last = None
                        for m in range(DC):
                            for kc in range(2):
                                last = te.matmul(
                                    PS[m][:, :],
                                    bigv(WDS + par * 2048 + kc * 1024 + m * 128, 128),
                                    bigv(HARE + e * 1024 + kc * 512, 512),
                                    start=(e == 0 and kc == 0),
                                    stop=(e == NEX - 1 and kc == 1))
                        return last
                    return f
                t_dmm[(Q, e)] = prog.emit("tensor", dmm(),
                                          wait=wl(t_wd, t_htok[(Q, e)], *w0))
            for m in range(DC):
                t_ffn_ev[Q][m] = prog.emit("vector",
                    (lambda m, Q: lambda v: v.tensor_tensor(
                        qfa[:, m * TC + Q * TS: m * TC + Q * TS + TS], PS[m][:, :],
                        qfa[:, m * TC + Q * TS: m * TC + Q * TS + TS], ALU.add))(m, Q),
                    wait=wl(t_dmm[(Q, NEX - 1)], t_accB[Q][m],
                            DBGACC[0] if Q == 0 else None))
                bank_free[m] = t_ffn_ev[Q][m]

        for e in range(0, 5):
            emit_expert(0, e)
        emit_kv_post()
        emit_phase_b()
        emit_rc_a(1)
        for e in range(5, 8):
            emit_expert(0, e)
        emit_rc_b(1)
        for e in range(8, NEX):
            emit_expert(0, e)
        emit_down(0)
        emit_rc_a(2)
        for e in range(0, 3):
            emit_expert(1, e)
        emit_rc_b(2)
        for e in range(3, NEX):
            emit_expert(1, e)
        emit_down(1)
        emit_rc_a(3)
        for e in range(0, 3):
            emit_expert(2, e)
        emit_rc_b(3)
        for e in range(3, NEX):
            emit_expert(2, e)
        emit_down(2)
        for e in range(NEX):
            emit_expert(3, e)
        emit_down(3)
        if debug:
            prog.emit("gpsimd", lambda g: dma(g, dbg["comb"][:, :],
                                              cfm[0:16, 0:TS]),
                      wait=[t_cfmQ[0]], inc=16)
            for c in range(DC):
                prog.emit("gpsimd", (lambda c: lambda g: dma(
                    g, dbg["ffn"][c], qfa[:, c * TC:(c + 1) * TC]))(c),
                    wait=[t_ffn_ev[3][c]], inc=16)

        if LVL < 5:
            return _finish(nc, prog, block, sems, debug)
        t_ytt = {}
        t_yst = {}
        for s in range(NSUB):
            for m in range(DC):
                idx = s * DC + m
                par = idx % 2
                t_xf = prog.emit("sync", (lambda s, m, par: lambda g: dma(
                    g, xf_sb[:, par * 512:(par + 1) * 512],
                    xd[s][m * 128:(m + 1) * 128, :]))(s, m, par),
                    wait=wl(t_ytt.get(idx - 2)), inc=16)
                t_ytt[idx] = prog.emit("vector",
                    (lambda s, m, par: lambda v: v.tensor_tensor(
                        y_sb[:, par * 512:(par + 1) * 512],
                        qfa[:, m * TC + s * TS: m * TC + s * TS + TS],
                        xf_sb[:, par * 512:(par + 1) * 512], ALU.add))(s, m, par),
                    wait=wl(t_xf, t_ffn_ev[s][m], t_yst.get(idx - 2)))
                t_yst[idx] = prog.emit("sync", (lambda s, m, par: lambda g: dma(
                    g, yd[s, m], y_sb[:, par * 512:(par + 1) * 512]))(s, m, par),
                    wait=[t_ytt[idx]], inc=16)

        return _finish(nc, prog, block, sems, debug)

    return nc


def _finish(nc, prog, block, sems, debug):
    qtot = prog.n["sync"]
    prog.emit("sync", lambda g: g.nop(), wait=[("sync", qtot)], inc=0)
    gtot = prog.n["gpsimd"]
    prog.emit("gpsimd", lambda g: g.nop(), wait=[("gpsimd", gtot)], inc=0)
    ttot = prog.n["tensor"]
    prog.emit("tensor", lambda t: t.nop(), wait=[("tensor", ttot)], inc=0)
    vtot = prog.n["vector"]
    prog.emit("vector", lambda v: v.nop(), wait=[("vector", vtot)], inc=0)
    stot = prog.n["scalar"]
    prog.emit("scalar", lambda sc: sc.nop(), wait=[("scalar", stot)], inc=0)

    def runner(eng):
        def body(EE):
            hw = {}
            for fn, waits, inc, coll in prog.ops[eng]:
                for (se, v) in waits:
                    if v > 0 and v > hw.get(se, 0):
                        EE.wait_ge(sems[se], v)
                        hw[se] = v
                inst = fn(EE)
                if coll is not None:
                    inst.then_inc(sems["coll"], 1)
                elif inc:
                    inst.then_inc(sems[eng], inc)
        return body

    block.tensor(runner("tensor"))
    block.vector(runner("vector"))
    block.scalar(runner("scalar"))
    block.sync(runner("sync"))
    block.gpsimd(runner("gpsimd"))
    return nc


_NC_CACHE = {}


def _get_nc(debug=False, upto="fin"):
    key = (bool(debug), upto)
    if key not in _NC_CACHE:
        _NC_CACHE[key] = build_program(debug=key[0], upto=upto)
    return _NC_CACHE[key]


def kernel(L0_attn_x, L0_attn_Wq, L0_attn_Wk, L0_attn_Wv, L0_attn_Wo, L0_attn_bo,
           L0_attn_Wf, L0_attn_bf, L0_ffn_router_w, L0_ffn_wg_r, L0_ffn_wu_r,
           L0_ffn_wd_r, L0_ffn_wg_s, L0_ffn_wu_s, L0_ffn_wd_s, _debug=False,
           _upto="fin"):
    cast = lambda a: np.ascontiguousarray(np.asarray(a, dtype=np.float32))
    x = cast(L0_attn_x)
    wq, wk, wv, wo = cast(L0_attn_Wq), cast(L0_attn_Wk), cast(L0_attn_Wv), cast(L0_attn_Wo)
    bo, wf, bf, rw = cast(L0_attn_bo), cast(L0_attn_Wf), cast(L0_attn_bf), cast(L0_ffn_router_w)
    wg_r, wu_r, wd_r = cast(L0_ffn_wg_r), cast(L0_ffn_wu_r), cast(L0_ffn_wd_r)
    wg_s, wu_s, wd_s = cast(L0_ffn_wg_s), cast(L0_ffn_wu_s), cast(L0_ffn_wd_s)
    perm = lambda a: np.ascontiguousarray(a)
    wg_r = perm(wg_r.reshape(NE, DC, 128, MIDR).transpose(0, 2, 1, 3)
                .reshape(NE, 128, DC * MIDR))
    wu_r = perm(wu_r.reshape(NE, DC, 128, MIDR).transpose(0, 2, 1, 3)
                .reshape(NE, 128, DC * MIDR))
    wd_r = perm(wd_r.reshape(NE, 2, 128, D).transpose(0, 2, 1, 3)
                .reshape(NE, 128, 2 * D))
    wg_s = perm(wg_s.reshape(2, DC, 128, 2, MIDR).transpose(0, 3, 2, 1, 4)
                .reshape(2, 2, 128, DC * MIDR))
    wu_s = perm(wu_s.reshape(2, DC, 128, 2, MIDR).transpose(0, 3, 2, 1, 4)
                .reshape(2, 2, 128, DC * MIDR))
    wd_s = perm(wd_s.reshape(2, 2, 2, 128, D).transpose(0, 1, 3, 2, 4)
                .reshape(2, 2, 128, 2 * D))

    nc = _get_nc(debug=_debug, upto=_upto)
    in_maps = []
    for c in range(8):
        b, half = c // 2, c % 2
        xc = x[b, half * TC:(half + 1) * TC, :]
        x_fm = np.ascontiguousarray(xc.T)
        xstg = np.ascontiguousarray(
            x_fm.reshape(D, NSUB, TS).transpose(1, 0, 2))
        in_maps.append({
            "x": xstg, "wq": wq, "wk": wk, "wv": wv, "wo": wo,
            "wf": wf, "bfv": bf, "bov": bo, "rw": rw,
            "wg_r": wg_r, "wu_r": wu_r, "wd_r": wd_r,
            "wg_s": wg_s, "wu_s": wu_s, "wd_s": wd_s,
        })
    res = run_bass_kernel_spmd(nc, in_maps, core_ids=list(range(8)))
    out = np.empty((B, S, D), dtype=np.float32)
    for c in range(8):
        b, half = c // 2, c % 2
        yc = np.asarray(res.results[c]["y"])
        y_fm = yc.transpose(1, 2, 0, 3).reshape(D, TC)
        out[b, half * TC:(half + 1) * TC, :] = y_fm.T
    if _debug:
        return out, res
    return out
